# version 09
# baseline (speedup 1.0000x reference)
import contextlib
import numpy as np
import concourse.bass as bass
import concourse.mybir as mybir
from concourse.bass_utils import run_bass_kernel_spmd

F32 = mybir.dt.float32
BF16 = mybir.dt.bfloat16
AF = mybir.ActivationFunctionType
ALU = mybir.AluOpType
AX = mybir.AxisListType.X

NCORES = 8
B_ALL, S, D = 16, 2048, 2048
BPC = B_ALL // NCORES
NT = BPC * S
DC = D // 128
NH, HD = 16, 128
FFN = 8192
FC = FFN // 128
KV1, HD1 = 4, 128
FD1 = 768
FC1 = FD1 // 128
NE, NR = 8, 8
MOE_SCALE = 16.0 / 8.0
EPS1 = 1e-8
LN_EPS = 1e-5
NEG_BIG = -1.0e9

ST = 512
COLL_CORES = NCORES
NST = NT // ST
INV_SQRT_HD = float(1.0 / np.sqrt(128.0))



def _rope_tables():
    half = HD // 2
    inv_freq = 1.0 / (10000.0 ** (np.arange(half, dtype=np.float64) / half))
    t = np.arange(S, dtype=np.float64)
    fr = inv_freq[:, None] * t[None, :]
    return (np.cos(fr).astype(np.float32), np.sin(fr).astype(np.float32),
            (np.cos(fr) * INV_SQRT_HD).astype(np.float32),
            (np.sin(fr) * INV_SQRT_HD).astype(np.float32))


def _perm_evens_odds():
    p = np.concatenate([np.arange(0, HD, 2), np.arange(1, HD, 2)])
    return np.concatenate([h * HD + p for h in range(NH)])


def _causal_masks():
    m = np.zeros((4, 128, 512), dtype=np.float32)
    qi = np.arange(128)[:, None]
    kr = np.arange(512)[None, :]
    for s_ in range(4):
        m[s_] = np.where(kr <= s_ * 128 + qi, 0.0, NEG_BIG)
    return m


def prep_host(inputs):
    g = {}
    perm = _perm_evens_odds()
    col = lambda b: np.ascontiguousarray(np.asarray(b).reshape(-1, 128).T)
    g["wqT"] = np.ascontiguousarray(np.asarray(inputs["L0_attn_Wq"])[perm, :].T)
    g["bq"] = col(np.asarray(inputs["L0_attn_bq"])[perm])
    g["wkT"] = np.ascontiguousarray(np.asarray(inputs["L0_attn_Wk"])[perm, :].T)
    g["bk"] = col(np.asarray(inputs["L0_attn_bk"])[perm])
    g["wvT"] = np.ascontiguousarray(np.asarray(inputs["L0_attn_Wv"]).T)
    g["bv"] = np.asarray(inputs["L0_attn_bv"])
    g["woT"] = np.ascontiguousarray(np.asarray(inputs["L0_attn_Wo"]).T)
    g["bo"] = col(inputs["L0_attn_bo"])
    g["w0T"] = np.ascontiguousarray(np.asarray(inputs["L0_ffn_W0"]).T)
    g["b0"] = col(inputs["L0_ffn_bias0"])
    g["w1T"] = np.ascontiguousarray(np.asarray(inputs["L0_ffn_W1"]).T)
    g["b1"] = col(inputs["L0_ffn_bias1"])
    g["wq1T"] = np.ascontiguousarray(np.asarray(inputs["L1_attn_Wq"]).T)
    g["bq1"] = col(inputs["L1_attn_bq"])
    g["wk1T"] = np.ascontiguousarray(np.asarray(inputs["L1_attn_Wk"]).T)
    g["bk1"] = col(inputs["L1_attn_bk"])
    g["wv1T"] = np.ascontiguousarray(np.asarray(inputs["L1_attn_Wv"]).T)
    g["bv1"] = col(inputs["L1_attn_bv"])
    g["wgT"] = np.ascontiguousarray(np.asarray(inputs["L1_attn_Wg"]).T)
    g["bg"] = col(inputs["L1_attn_bg"])
    g["wmbT"] = np.ascontiguousarray(np.asarray(inputs["L1_attn_Wmb"]).T)
    g["bmb"] = col(inputs["L1_attn_bmb"])
    g["mem0"] = np.asarray(inputs["L1_attn_memory"])
    g["norm0T"] = np.ascontiguousarray(np.asarray(inputs["L1_attn_norm"]).T)
    g["wbT"] = np.ascontiguousarray(np.asarray(inputs["L1_ffn_W_base"]).T)
    g["bb"] = col(inputs["L1_ffn_b_base"])
    g["wrT"] = np.ascontiguousarray(np.asarray(inputs["L1_ffn_W_router"]).T)
    aw = np.asarray(inputs["L1_ffn_A_w"])
    g["waT"] = np.ascontiguousarray(aw.reshape(NE * NR, D).T)
    bw = np.asarray(inputs["L1_ffn_B_w"])
    g["wbcT"] = np.ascontiguousarray(bw.transpose(0, 2, 1).reshape(NE * NR, D))
    cos, sin, cq, sq = _rope_tables()
    g["c_k"] = cos; g["s_k"] = sin; g["c_q"] = cq; g["s_q"] = sq
    g["ident"] = np.eye(128, dtype=np.float32)
    g["cmask"] = _causal_masks()
    g["oh864"] = np.kron(np.eye(NE), np.ones((1, NR))).astype(np.float32)

    x = np.asarray(inputs["L0_attn_x"])
    per_core = []
    for c in range(NCORES):
        xc = x[c * BPC:(c + 1) * BPC]
        xfm = np.ascontiguousarray(xc.reshape(NT, D).T).reshape(DC, 128, NT)
        per_core.append({"x_fm": xfm})
    return g, per_core



SEM_NAMES = ("t", "v", "s", "g", "dS", "dG", "dC", "cc")
DMA_SEMS = {"dS": "sync", "dG": "gpsimd", "dC": "scalar"}
ENG_SEM = {"tensor": "t", "vector": "v", "scalar": "s", "gpsimd": "g"}


def fm3(dram_ap, t0, t1, c0=0, c1=None):
    if c1 is None:
        c1 = dram_ap.shape[0]
    return dram_ap[c0:c1, :, t0:t1].transpose([1, 0, 2])


def sb3(sbuf_ap, nch, width=None):
    return sbuf_ap.rearrange("p (c t) -> p c t", c=nch)


def merge_w(*wait_lists):
    d = {}
    for wl in wait_lists:
        for (s_, v_) in wl:
            cur = d.get(s_, 0)
            if cur is None:
                continue
            if v_ is None or v_ > cur:
                d[s_] = v_
    return list(d.items())


class Prog:

    def __init__(self):
        self.stream = {e: [] for e in ("sync", "tensor", "vector", "scalar", "gpsimd")}
        self.n = {k: 0 for k in SEM_NAMES}
        self.handles = {}
        self.init_waits = []
        self.dma_hist = {}

    def op(self, eng, fname, *args, waits=(), inc=None, incby=None, chain=False, **kw):
        waits = [(s_, (self.n[s_] if v_ is None else v_)) for (s_, v_) in waits]
        waits = [(s_, v_) for (s_, v_) in waits if v_ > 0]
        if chain:
            own = ENG_SEM[eng]
            waits = merge_w(waits, [(own, self.n[own])])
        if inc is not None:
            if incby is None:
                incby = 16 if inc in DMA_SEMS else 1
            self.n[inc] += incby
        waits = tuple(waits)

        def fn(e, args=args, kw=kw, waits=waits, inc=inc, incby=incby):
            for (s_, v_) in waits:
                e.wait_ge(self.handles[s_], v_)
            ins = getattr(e, fname)(*args, **kw)
            if inc is not None:
                ins.then_inc(self.handles[inc], incby)
        self.stream[eng].append(fn)
        return self.n[inc] if inc is not None else None

    def dma(self, eng, out, in_, waits=(), **kw):
        sem = {"sync": "dS", "gpsimd": "dG", "scalar": "dC"}[eng]
        return self.op(eng, "dma_start", waits=waits, inc=sem, out=out, in_=in_, **kw)

    def barrier(self):
        for eng in ("tensor", "vector", "scalar", "gpsimd"):
            self.op(eng, "nop", chain=True, inc=ENG_SEM[eng])
        vals = dict(self.n)
        wl = tuple((k, v) for k, v in vals.items() if v > 0)
        for eng in ("sync", "tensor", "vector", "scalar", "gpsimd"):
            def fn(e, wl=wl):
                for (s_, v_) in wl:
                    e.wait_ge(self.handles[s_], v_)
            self.stream[eng].append(fn)
        return vals

    def realize(self, nc, es):
        for k in SEM_NAMES:
            self.handles[k] = es.enter_context(nc.semaphore(k))
        blk = es.enter_context(nc.Block())
        iw = tuple(self.init_waits)
        for eng in ("sync", "tensor", "vector", "scalar", "gpsimd"):
            ops = self.stream[eng]
            if not ops:
                continue

            def run(e, ops=ops, eng=eng):
                if eng != "vector":
                    for (s_, v_) in iw:
                        e.wait_ge(self.handles[s_], v_)
                for f in ops:
                    f(e)
            getattr(blk, eng)(run)


SCRATCH = {
    "xn":  ([DC, 128, NT], BF16),
    "q0":  ([NH, 128, NT], BF16),
    "k0":  ([NH, 128, NT], BF16),
    "v0":  ([NT, D], BF16),
    "o0":  ([DC, 128, NT], BF16),
    "h0":  ([FC, 128, NT], BF16),
    "w1b": ([FC, 128, D], BF16),
    "x1":  ([DC, 128, NT], F32),
    "xn1": ([DC, 128, NT], BF16),
    "q1":  ([NH, 128, NT], BF16),
    "k1":  ([KV1, 128, NT], BF16),
    "v1":  ([KV1, 128, NT], BF16),
    "mb1": ([NH, 128, NT], BF16),
    "g1":  ([DC, 128, NT], BF16),
    "lg1": ([8, NT], F32),
    "mh1": ([NH, 128, NT], BF16),
    "dh1": ([NH, NT], F32),
    "drcp": ([NH, NT], F32),
    "nmem": ([128, NH * FC1 * 128], BF16),
    "nnr": ([128, NH * FC1], BF16),
    "ccmi": ([128, NH * FD1], F32),
    "ccmo": ([128, NH * FD1], F32),
    "ccni": ([NH * FD1], F32),
    "ccno": ([NH * FD1], F32),
}

GLOBALS_SHAPE = {
    "x_fm": ([DC, 128, NT], F32),
    "wqT": ([D, D], F32), "bq": ([128, NH], F32),
    "wkT": ([D, D], F32), "bk": ([128, NH], F32),
    "wvT": ([D, D], F32), "bv": ([D], F32),
    "woT": ([D, D], F32), "bo": ([128, DC], F32),
    "w0T": ([D, FFN], F32), "b0": ([128, FC], F32),
    "w1T": ([FFN, D], F32), "b1": ([128, DC], F32),
    "wq1T": ([D, D], F32), "bq1": ([128, NH], F32),
    "wk1T": ([D, KV1 * HD1], F32), "bk1": ([128, KV1], F32),
    "wv1T": ([D, KV1 * HD1], F32), "bv1": ([128, KV1], F32),
    "wgT": ([D, D], F32), "bg": ([128, DC], F32),
    "wmbT": ([D, D], F32), "bmb": ([128, NH], F32),
    "mem0": ([16, FD1, HD1], F32), "norm0T": ([FD1, 16], F32),
    "wbT": ([D, D], F32), "bb": ([128, DC], F32),
    "wrT": ([D, NE], F32),
    "waT": ([D, NE * NR], F32),
    "wbcT": ([NE * NR, D], F32),
    "c_k": ([64, S], F32), "s_k": ([64, S], F32),
    "c_q": ([64, S], F32), "s_q": ([64, S], F32),
    "ident": ([128, 128], F32),
    "cmask": ([4, 128, 512], F32),
    "oh864": ([NE, NE * NR], F32),
}

PHASE_USES = {
    "A0": ["x_fm"],
    "A1a": ["wqT", "bq", "wkT", "bk", "c_q", "s_q", "c_k", "s_k"],
    "A1b": ["wvT", "bv"],
    "B": ["woT", "bo", "ident", "cmask"],
    "C1": ["w0T", "b0", "w1T"],
    "C2": ["b1", "x_fm"],
    "D": ["wq1T", "bq1", "wk1T", "bk1", "wv1T", "bv1", "wgT", "bg", "wmbT", "bmb",
          "wrT"],
    "E": ["mem0", "norm0T", "ident"],
    "F": ["mem0", "norm0T", "ident"],
    "G": ["wbT", "bb", "waT", "wbcT", "oh864", "ident"],
}


class Ctx:
    def __init__(self, nc, es, dram):
        self.nc = nc
        self.es = es
        self.dram = dram
        self.p = Prog()
        self.ps_S = es.enter_context(nc.psum_tensor("ps_S", [128, 2048], F32))
        self.ps_A = es.enter_context(nc.psum_tensor("ps_A", [128, 512], F32))
        self.ps_B = es.enter_context(nc.psum_tensor("ps_B", [128, 512], F32))
        self.ps_C = es.enter_context(nc.psum_tensor("ps_C", [128, 512], F32))
        self.ps_BF = es.enter_context(nc.psum_tensor("ps_BF", [128, 1024], BF16))
        self.ones_bf = es.enter_context(nc.sbuf_tensor("ones_bf", [128, 1], BF16))
        self.ones_f = es.enter_context(nc.sbuf_tensor("ones_f", [1, 128], F32))
        p = self.p
        p.op("vector", "memset", self.ones_bf[:, :], 1.0, inc="v")
        p.op("vector", "memset", self.ones_f[:, :], 1.0, inc="v")
        p.init_waits = [("v", p.n["v"])]



def emit_A0(cx, src_name="x_fm", dst_name="xn"):
    p, nc = cx.p, cx.nc
    x_fm, xn = cx.dram[src_name], cx.dram[dst_name]
    pfx = "a0" + dst_name
    with contextlib.ExitStack() as es:
        sb = lambda n, sh, dt: es.enter_context(nc.sbuf_tensor(pfx + n[2:], sh, dt))
        xf = [sb(f"a0_xf{i}", [128, DC * ST], F32) for i in range(2)]
        xc = [sb(f"a0_xc{i}", [128, DC * ST], BF16) for i in range(2)]
        xq = [sb(f"a0_xq{i}", [128, DC * ST], BF16) for i in range(2)]
        xo = [sb(f"a0_xo{i}", [128, DC * ST], BF16) for i in range(2)]
        rowsA = [sb(f"a0_rows{i}", [1, 8 * ST], F32) for i in range(2)]
        b1s2 = [sb(f"a0_b1s{i}", [128, ST], F32) for i in range(2)]
        b2s2 = [sb(f"a0_b2s{i}", [128, ST], F32) for i in range(2)]

        war = {k: {} for k in ("xf", "xc", "xq", "xo")}
        prev = {}
        for i in range(NST):
            par = i % 2
            t0 = i * ST
            rows = rowsA[par]
            b1s, b2s = b1s2[par], b2s2[par]
            mean = rows[0:1, 0 * ST:1 * ST]
            e2p = rows[0:1, 1 * ST:2 * ST]
            m2 = rows[0:1, 2 * ST:3 * ST]
            varp = rows[0:1, 3 * ST:4 * ST]
            sstd = rows[0:1, 4 * ST:5 * ST]
            rstd = rows[0:1, 5 * ST:6 * ST]
            nmr = rows[0:1, 6 * ST:7 * ST]
            ps_sum = cx.ps_A[0:1, 0:ST] if par == 0 else cx.ps_B[0:1, 0:ST]
            ps_sq = cx.ps_C[0:1, 0:ST] if par == 0 else cx.ps_S[0:1, 0:ST]
            ps_b1 = cx.ps_S[:, 512:1024] if par == 0 else cx.ps_S[:, 1536:2048]
            ps_b2 = cx.ps_S[:, 1024:1536] if par == 0 else cx.ps_A[:, 0:512]
            src = fm3(x_fm, t0, t0 + ST)
            ldf = p.dma("sync", sb3(xf[par][:, :], DC), src, waits=war["xf"].get(par, []))
            ldc = p.dma("gpsimd", sb3(xc[par][:, :], DC), src, waits=war["xc"].get(par, []))
            sq = p.op("scalar", "activation", xq[par][:, :], xc[par][:, :], AF.Square,
                      waits=merge_w([("dG", ldc)], war["xq"].get(par, [])),
                      chain=True, inc="s")
            for c in range(DC):
                p.op("tensor", "matmul", ps_sum, cx.ones_bf[:, 0:1],
                     xc[par][:, c * ST:(c + 1) * ST],
                     waits=(merge_w([("dG", ldc)], prev.get(("st", par), []))
                            if c == 0 else []),
                     start=(c == 0), stop=(c == DC - 1),
                     inc=("t" if c == DC - 1 else None))
            tsum = p.n["t"]
            for c in range(DC):
                p.op("tensor", "matmul", ps_sq, cx.ones_bf[:, 0:1],
                     xq[par][:, c * ST:(c + 1) * ST],
                     waits=([("s", sq)] if c == 0 else []),
                     start=(c == 0), stop=(c == DC - 1),
                     inc=("t" if c == DC - 1 else None))
            tsq = p.n["t"]
            war["xc"][par] = [("t", tsum), ("s", sq)]
            war["xq"][par] = [("t", tsq)]
            p.op("vector", "tensor_scalar_mul", mean, ps_sum, 1.0 / D,
                 waits=merge_w([("t", tsum)], prev.get(("rows", par), [])), inc="v")
            p.op("vector", "tensor_scalar", e2p, ps_sq, 1.0 / D, LN_EPS,
                 ALU.mult, ALU.add, waits=[("t", tsq)], inc="v")
            p.op("vector", "tensor_mul", m2, mean, mean, chain=True, inc="v")
            vv = p.op("vector", "tensor_sub", varp, e2p, m2, chain=True, inc="v")
            sv = p.op("scalar", "activation", sstd, varp, AF.Sqrt,
                      waits=[("v", vv)], chain=True, inc="s")
            rv = p.op("vector", "reciprocal", rstd, sstd, waits=[("s", sv)], inc="v")
            p.op("gpsimd", "tensor_mul", nmr, mean, rstd,
                 waits=[("v", rv), ("g", p.n["g"])], inc="g")
            gn = p.op("gpsimd", "tensor_scalar_mul", nmr, nmr, -1.0, chain=True, inc="g")
            prev[("rows", par)] = [("g", gn)]
            tb1 = p.op("tensor", "matmul", ps_b1, cx.ones_f[0:1, :], rstd,
                       waits=merge_w([("v", rv)], prev.get(("bc", par), [])),
                       start=True, stop=True, inc="t")
            tb2 = p.op("tensor", "matmul", ps_b2, cx.ones_f[0:1, :], nmr,
                       waits=[("g", gn)], start=True, stop=True, inc="t")
            e1 = p.op("vector", "tensor_copy", b1s[:, :], ps_b1,
                      waits=[("t", tb1)], inc="v")
            e2 = p.op("vector", "tensor_copy", b2s[:, :], ps_b2,
                      waits=[("t", tb2)], chain=True, inc="v")
            prev[("bc", par)] = [("v", e2)]
            prev[("st", par)] = [("v", e2)]
            for c in range(DC):
                eng = "vector" if c % 2 == 0 else "gpsimd"
                w0 = []
                if c < 2:
                    w0 = merge_w([("dS", ldf), ("v", e2)], war["xo"].get(par, []))
                p.op(eng, "tensor_mul", xo[par][:, c * ST:(c + 1) * ST],
                     xf[par][:, c * ST:(c + 1) * ST], b1s[:, :],
                     waits=w0, inc=ENG_SEM[eng])
                p.op(eng, "tensor_tensor", xo[par][:, c * ST:(c + 1) * ST],
                     xo[par][:, c * ST:(c + 1) * ST], b2s[:, :], ALU.add,
                     chain=True, inc=ENG_SEM[eng])
            fin = [("v", p.n["v"]), ("g", p.n["g"])]
            war["xf"][par] = fin
            dst = fm3(xn, t0, t0 + ST)
            stv = p.dma("sync", dst, sb3(xo[par][:, :], DC), waits=fin)
            war["xo"][par] = [("dS", None)]
        p.barrier()


def _emit_proj_featmajor(cx, name, wT_name, bias_name, out_name, nheads,
                         rope=None, epilogue="bias", xn_name="xn",
                         kchunks=DC, mstart=0, scale=None, per_m_hook=None):
    p, nc = cx.p, cx.nc
    wT, out_d = cx.dram[wT_name], cx.dram[out_name]
    with contextlib.ExitStack() as es:
        sb = lambda n, sh, dt: es.enter_context(nc.sbuf_tensor(n, sh, dt))
        xr = sb(f"{name}_x", [128, kchunks * NT], BF16)
        wb = [sb(f"{name}_w{i}", [128, kchunks * 128], BF16) for i in range(2)]
        ev = [sb(f"{name}_e{i}", [128, NT], BF16) for i in range(2)]
        ro = [sb(f"{name}_r{i}", [128, NT], BF16) for i in range(2)] if rope else None
        bias = sb(f"{name}_b", [128, nheads], F32)
        ctv = stv_ = 0
        if rope:
            ct = sb(f"{name}_c", [128, S], BF16)
            st_ = sb(f"{name}_s", [128, S], BF16)
            csrc = cx.dram["c_q" if rope == "q" else "c_k"]
            ssrc = cx.dram["s_q" if rope == "q" else "s_k"]
            p.dma("gpsimd", ct[0:64, :], csrc[:, :])
            p.dma("gpsimd", ct[64:128, :], csrc[:, :])
            p.dma("gpsimd", st_[0:64, :], ssrc[:, :])
            stv_ = p.dma("gpsimd", st_[64:128, :], ssrc[:, :])
            sc = sb(f"{name}_sc", [128, ST], BF16)
        ldb = p.dma("sync", bias[:, :], cx.dram[bias_name][:, mstart:mstart + nheads])
        ldx = p.dma("sync", sb3(xr[:, :], kchunks), fm3(cx.dram[xn_name], 0, NT))
        ep_eng = "scalar" if epilogue in ("sigmoid", "gelu") else "vector"
        ep_sem = ENG_SEM[ep_eng]
        w_war = {}
        ps_war = {}
        ev_war = {}
        ro_war = {}
        prev_mm = 0
        for m in range(nheads):
            par = m % 2
            m0 = (mstart + m) * 128
            wsrc = wT[:, m0:m0 + 128].rearrange("(c p) m -> p c m", p=128)
            ldw = p.dma("gpsimd", sb3(wb[par][:, :], kchunks), wsrc,
                        waits=merge_w(w_war.get(par, []), [("t", prev_mm)]))
            if per_m_hook is not None:
                per_m_hook(m)
            prev_mm = p.n["t"] + 1
            for i in range(NST):
                t0 = i * ST
                pb = i % 2
                psl = cx.ps_S[:, pb * 512:pb * 512 + 512]
                for c in range(kchunks):
                    w0 = []
                    if c == 0:
                        w0 = merge_w([("dG", None), ("dS", ldx)], ps_war.get(pb, []))
                    p.op("tensor", "matmul", psl, wb[par][:, c * 128:(c + 1) * 128],
                         xr[:, c * NT + t0:c * NT + t0 + ST], waits=w0,
                         start=(c == 0), stop=(c == kchunks - 1),
                         inc=("t" if c == kchunks - 1 else None))
                tmm = p.n["t"]
                w_war[par] = [("t", tmm)]
                dst = ev[par][:, t0:t0 + ST]
                wts = merge_w([("t", tmm), ("dS", None)], ev_war.get(par, []))
                if epilogue == "sigmoid":
                    epv = p.op("scalar", "activation", dst, psl, AF.Sigmoid,
                               bias=bias[:, m:m + 1], waits=wts, inc="s")
                elif epilogue == "gelu":
                    epv = p.op("scalar", "activation", dst, psl, AF.Gelu,
                               bias=bias[:, m:m + 1], waits=wts, inc="s")
                else:
                    epv = p.op("vector", "tensor_scalar_add", dst, psl,
                               bias[:, m:m + 1], waits=wts, inc=ep_sem)
                ps_war[pb] = [(ep_sem, epv)]
                if rope:
                    s0 = t0 % S
                    top_i = ev[par][0:64, t0:t0 + ST]
                    bot_i = ev[par][64:128, t0:t0 + ST]
                    top_o = ro[par][0:64, t0:t0 + ST]
                    bot_o = ro[par][64:128, t0:t0 + ST]
                    w_first = merge_w([(ep_sem, epv), ("dG", None)],
                                      ro_war.get(par, []))
                    v1 = p.op("vector", "tensor_mul", sc[0:64, :], bot_i,
                              st_[64:128, s0:s0 + ST], waits=w_first, chain=True,
                              inc="v")
                    p.op("vector", "tensor_mul", top_o, top_i,
                         ct[0:64, s0:s0 + ST], inc="v")
                    rv = p.op("vector", "tensor_sub", top_o, top_o, sc[0:64, :],
                              chain=True, inc="v")
                    p.op("gpsimd", "tensor_mul", sc[64:128, :], top_i,
                         st_[0:64, s0:s0 + ST], waits=w_first, chain=True, inc="g")
                    p.op("gpsimd", "tensor_mul", bot_o, bot_i,
                         ct[64:128, s0:s0 + ST], inc="g")
                    rg = p.op("gpsimd", "tensor_add", bot_o, bot_o, sc[64:128, :],
                              chain=True, inc="g")
            if rope:
                rope_f = [("v", p.n["v"]), ("g", p.n["g"])]
                ev_war[par] = rope_f
                stw = p.dma("sync", out_d[mstart + m, :, :], ro[par][:, :],
                            waits=rope_f)
                ro_war[par] = [("dS", None)]
            else:
                stw = p.dma("sync", out_d[mstart + m, :, :], ev[par][:, :],
                            waits=[(ep_sem, p.n[ep_sem])])
                ev_war[par] = [("dS", None)]
        p.barrier()


def emit_A1a(cx):
    _emit_proj_featmajor(cx, "qp", "wqT", "bq", "q0", NH, rope="q")
    _emit_proj_featmajor(cx, "kp", "wkT", "bk", "k0", NH, rope="k")


def emit_A1b(cx):
    p, nc = cx.p, cx.nc
    with contextlib.ExitStack() as es:
        sb = lambda n, sh, dt: es.enter_context(nc.sbuf_tensor(n, sh, dt))
        wv = sb("vp_w", [128, DC * D], BF16)
        xs = [sb(f"vp_x{i}", [128, DC * 128], BF16) for i in range(2)]
        evb = [sb(f"vp_e{i}", [128, D], BF16) for i in range(2)]
        bias = sb("vp_b", [1, D], F32)
        bbc = sb("vp_bb", [128, D], F32)
        ldw = p.dma("gpsimd", sb3(wv[:, :], DC),
                    cx.dram["wvT"].rearrange("(c p) m -> p c m", p=128))
        ldb = p.dma("sync", bias[0:1, :], cx.dram["bv"].rearrange("(o d) -> o d", o=1))
        bcv = 0
        for nb in range(4):
            tv = p.op("tensor", "matmul", cx.ps_C[:, 0:512], cx.ones_f[0:1, :],
                      bias[0:1, nb * 512:(nb + 1) * 512],
                      waits=[("dS", ldb), ("v", bcv)], start=True, stop=True, inc="t")
            bcv = p.op("vector", "tensor_copy", bbc[:, nb * 512:(nb + 1) * 512],
                       cx.ps_C[:, 0:512], waits=[("t", tv)], inc="v")
        vd = cx.dram["v0"]
        xn_d = cx.dram["xn"]
        x_war = {0: [("v", bcv)], 1: [("v", bcv)]}
        prev_ev = 0
        for i in range(NT // 128):
            par = i % 2
            t0 = i * 128
            xsrc = fm3(xn_d, t0, t0 + 128)
            ldx = p.dma("sync", sb3(xs[par][:, :], DC), xsrc, waits=x_war.get(par, []))
            for c in range(DC):
                for nb in range(4):
                    w0 = []
                    if c == 0 and nb == 0:
                        w0 = [("dG", ldw), ("dS", ldx), ("v", prev_ev)]
                    p.op("tensor", "matmul", cx.ps_S[:, nb * 512:(nb + 1) * 512],
                         xs[par][:, c * 128:(c + 1) * 128],
                         wv[:, c * D + nb * 512:c * D + (nb + 1) * 512],
                         waits=w0, start=(c == 0), stop=(c == DC - 1),
                         inc=("t" if (c == DC - 1 and nb == 3) else None))
            tmm = p.n["t"]
            x_war[par] = [("t", tmm)]
            prev_ev = p.op("vector", "scalar_tensor_tensor", evb[par][:, :],
                           cx.ps_S[:, 0:2048], 1.0, bbc[:, :], ALU.mult, ALU.add,
                           waits=merge_w([("t", tmm), ("v", bcv)],
                                         x_war.get(("e", par), [])), inc="v")
            stw = p.dma("sync", vd[t0:t0 + 128, :], evb[par][:, :],
                        waits=[("v", prev_ev)])
            x_war[("e", par)] = [("dS", None)]
        p.barrier()




def emit_B(cx):
    p, nc = cx.p, cx.nc
    q0d, k0d, v0d, o0d = cx.dram["q0"], cx.dram["k0"], cx.dram["v0"], cx.dram["o0"]
    with contextlib.ExitStack() as es:
        sb = lambda n, sh, dt: es.enter_context(nc.sbuf_tensor(n, sh, dt))
        qh = [sb(f"b_q{i}", [128, S], BF16) for i in range(2)]
        kh = [sb(f"b_k{i}", [128, S], BF16) for i in range(2)]
        vh = [sb(f"b_v{i}", [128, 16 * 128], BF16) for i in range(2)]
        Pt = sb("b_P", [128, 4 * S], BF16)
        PTs = [sb(f"b_pt{i}", [128, 512], BF16) for i in range(2)]
        nmax_t = sb("b_nmax", [128, 1], F32)
        rowsum = sb("b_rsum", [128, 4], F32)
        rsr = sb("b_rsr", [128, 4], F32)
        rsrow = sb("b_rsrow", [1, 512], BF16)
        bcs = sb("b_bcs", [128, 512], BF16)
        cm = sb("b_cm", [128, 4 * 512], F32)
        idn = sb("b_idn", [128, 128], BF16)
        idnf = sb("b_idnf", [128, 128], F32)
        A_seq = sb("b_A", [128, NH * S], BF16)
        wo_b = [sb(f"b_wo{i}", [128, DC * 128], BF16) for i in range(2)]
        o_ev = [sb(f"b_oe{i}", [128, S], BF16) for i in range(2)]
        bo_t = sb("b_bo", [128, DC], F32)
        ones_row = sb("b_ones", [1, 128], BF16)

        p.op("vector", "memset", ones_row[:, :], 1.0, inc="v")
        p.dma("sync", sb3(cm[:, :], 4), cx.dram["cmask"].transpose([1, 0, 2]))
        p.dma("gpsimd", idn[:, :], cx.dram["ident"][:, :])
        p.dma("sync", idnf[:, :], cx.dram["ident"][:, :])
        p.dma("sync", bo_t[:, :], cx.dram["bo"][:, :])

        ld_war = {}
        w_war = {}
        oe_war = {}
        last_exp = 0
        pt_read_done = 0
        bf_evac = {}
        bc_evac = 0
        att_evac = 0
        aseq_war = 0
        pts_war = {}

        prev_unit_mm = 0
        for b in range(BPC):
            for h in range(NH):
                par = (b * NH + h) % 2
                t0b = b * S
                p.dma("sync", qh[par][:, :], q0d[h, :, t0b:t0b + S],
                      waits=merge_w(ld_war.get(par, []), [("t", prev_unit_mm)]))
                p.dma("sync", kh[par][:, :], k0d[h, :, t0b:t0b + S])
                p.dma("sync", sb3(vh[par][:, :], 16),
                      v0d[t0b:t0b + S, h * 128:(h + 1) * 128]
                      .rearrange("(kt p) d -> p kt d", p=128))
                prev_unit_mm = p.n["t"] + 1
                for qb in range(4):
                    width = (qb + 1) * 512
                    nkt = (qb + 1) * 4
                    sexp = {}
                    vrse = 0
                    for qs in range(4):
                        qoff = qb * 512 + qs * 128
                        for seg in range(qb + 1):
                            w0 = []
                            if seg == 0:
                                w0 = [("dS", None), ("s", last_exp)]
                            p.op("tensor", "matmul",
                                 cx.ps_S[:, seg * 512:(seg + 1) * 512],
                                 qh[par][:, qoff:qoff + 128],
                                 kh[par][:, seg * 512:(seg + 1) * 512],
                                 waits=w0, start=True, stop=True,
                                 inc=("t" if seg == qb else None))
                        tsc = p.n["t"]
                        vmask = p.op("vector", "tensor_tensor",
                                     cx.ps_S[:, qb * 512:qb * 512 + 512],
                                     cx.ps_S[:, qb * 512:qb * 512 + 512],
                                     cm[:, qs * 512:(qs + 1) * 512], ALU.add,
                                     waits=[("t", tsc), ("s", last_exp)], inc="v")
                        vmax = p.op("vector", "reduce_max", nmax_t[:, 0:1],
                                    cx.ps_S[:, 0:width], axis=AX, negate=True,
                                    chain=True, inc="v")
                        last_exp = sexp[qs] = p.op(
                            "scalar", "activation", Pt[:, qs * S:qs * S + width],
                            cx.ps_S[:, 0:width], AF.Exp, bias=nmax_t[:, 0:1],
                            accum_out=rowsum[:, qs:qs + 1],
                            waits=[("v", vmax), ("t", pt_read_done)], inc="s")
                        vr = p.op("vector", "reciprocal", rsr[:, qs:qs + 1],
                                  rowsum[:, qs:qs + 1],
                                  waits=[("s", sexp[qs])], inc="v")
                        tr = p.op("tensor", "transpose",
                                  cx.ps_C[0:1, qs * 128:(qs + 1) * 128],
                                  rsr[:, qs:qs + 1], idnf[:, :],
                                  waits=[("v", vr), ("v", bc_evac), ("dS", None)],
                                  inc="t")
                        vrse = p.op("vector", "tensor_copy",
                                    rsrow[0:1, qs * 128:(qs + 1) * 128],
                                    cx.ps_C[0:1, qs * 128:(qs + 1) * 128],
                                    waits=[("t", tr)], inc="v")
                    tbc = p.op("tensor", "matmul", cx.ps_C[:, 0:512],
                               ones_row[0:1, :], rsrow[0:1, :],
                               waits=[("v", vrse)], start=True, stop=True, inc="t")
                    bc_evac = p.op("vector", "tensor_copy", bcs[:, :],
                                   cx.ps_C[:, 0:512], waits=[("t", tbc)],
                                   chain=True, inc="v")
                    tav = 0
                    for kt in range(nkt):
                        ptp = kt % 2
                        for qs in range(4):
                            w0 = []
                            if qs == 0:
                                w0 = [("s", sexp[3]), ("v", bf_evac.get(ptp, 0)),
                                      ("dG", None)]
                            p.op("tensor", "transpose",
                                 cx.ps_BF[:, ptp * 512 + qs * 128:
                                          ptp * 512 + (qs + 1) * 128],
                                 Pt[:, qs * S + kt * 128:qs * S + (kt + 1) * 128],
                                 idn[:, :], waits=w0,
                                 inc=("t" if qs == 3 else None))
                        ttr = p.n["t"]
                        bf_evac[ptp] = p.op("vector", "tensor_copy",
                                            PTs[ptp][:, :],
                                            cx.ps_BF[:, ptp * 512:(ptp + 1) * 512],
                                            waits=[("t", ttr),
                                                   ("t", pts_war.get(ptp, 0))],
                                            inc="v")
                        w0 = [("v", bf_evac[ptp])]
                        if kt == 0:
                            w0.append(("v", att_evac))
                        tav = p.op("tensor", "matmul", cx.ps_A[:, 0:512],
                                   vh[par][:, kt * 128:(kt + 1) * 128],
                                   PTs[ptp][:, :], waits=w0,
                                   start=(kt == 0), stop=(kt == nkt - 1), inc="t")
                        pts_war[ptp] = tav
                    att_evac = p.op("vector", "scalar_tensor_tensor",
                                    A_seq[:, h * S + qb * 512:h * S + qb * 512 + 512],
                                    cx.ps_A[:, 0:512], 1.0, bcs[:, :],
                                    ALU.mult, ALU.mult,
                                    waits=[("t", tav), ("t", aseq_war)],
                                    chain=True, inc="v")
                    pt_read_done = p.n["t"]
                ld_war[par] = [("t", p.n["t"])]
            aseq_ready = p.n["v"]
            vev = 0
            prev_wo_mm = 0
            for m in range(DC):
                wpar = m % 2
                p.dma("gpsimd", sb3(wo_b[wpar][:, :], DC),
                      cx.dram["woT"][:, m * 128:(m + 1) * 128]
                      .rearrange("(c p) m -> p c m", p=128),
                      waits=merge_w(w_war.get(wpar, []), [("t", prev_wo_mm)],
                                    [("v", aseq_ready)] if m == 0 else []))
                prev_wo_mm = p.n["t"] + 1
                for st in range(4):
                    pb = st % 2
                    psl = cx.ps_S[:, pb * 512:pb * 512 + 512]
                    for c in range(DC):
                        w0 = []
                        if c == 0:
                            w0 = [("dG", None), ("v", aseq_ready),
                                  ("v", oe_war.get(pb, 0)),
                                  ("s", last_exp)]
                        p.op("tensor", "matmul", psl,
                             wo_b[wpar][:, c * 128:(c + 1) * 128],
                             A_seq[:, c * S + st * 512:c * S + st * 512 + 512],
                             waits=w0, start=(c == 0), stop=(c == DC - 1),
                             inc=("t" if c == DC - 1 else None))
                    tmm = p.n["t"]
                    vev = p.op("vector", "tensor_scalar_add",
                               o_ev[wpar][:, st * 512:(st + 1) * 512], psl,
                               bo_t[:, m:m + 1],
                               waits=[("t", tmm), ("dS", None)], inc="v")
                    oe_war[pb] = vev
                w_war[wpar] = [("t", p.n["t"])]
                p.dma("sync", o0d[m, :, b * S:(b + 1) * S], o_ev[wpar][:, :],
                      waits=[("v", vev)])
            aseq_war = p.n["t"]
            woev = p.n["v"]
            for pk in (0, 1):
                ld_war[pk] = merge_w(ld_war.get(pk, []), [("v", woev)])
        p.barrier()




def emit_C1(cx):
    p, nc = cx.p, cx.nc
    with contextlib.ExitStack() as es:
        wstage = [es.enter_context(nc.sbuf_tensor(f"c1_ws{i}", [128, D], BF16))
                  for i in range(2)]
        st_war = {}

        def precast(m):
            par = m % 2
            ld = p.dma("gpsimd", wstage[par][:, :],
                       cx.dram["w1T"][m * 128:(m + 1) * 128, :],
                       waits=st_war.get(par, []))
            stv = p.dma("gpsimd", cx.dram["w1b"][m, :, :], wstage[par][:, :],
                        waits=[("dG", ld)])
            st_war[par] = [("dG", stv)]
        _emit_proj_featmajor(cx, "ffa", "w0T", "b0", "h0", FC,
                             epilogue="gelu", per_m_hook=precast)


def emit_C2(cx):
    p, nc = cx.p, cx.nc
    SB = 512
    with contextlib.ExitStack() as es:
        sb = lambda n, sh, dt: es.enter_context(nc.sbuf_tensor(n, sh, dt))
        hres = [sb(f"c2_h{i}", [128, FC * SB], BF16) for i in range(2)]
        w1l = [sb(f"c2_w{i}", [128, FC * 128], BF16) for i in range(2)]
        xf = [sb(f"c2_x{i}", [128, 2 * SB], F32) for i in range(2)]
        of = [sb(f"c2_o{i}", [128, 2 * SB], BF16) for i in range(2)]
        xo = [sb(f"c2_xo{i}", [128, 2 * SB], F32) for i in range(2)]
        x1t = [sb(f"c2_x1{i}", [128, 2 * SB], F32) for i in range(2)]
        b1t = sb("c2_b1", [128, DC], F32)
        p.dma("sync", b1t[:, :], cx.dram["b1"][:, :])
        w_war = {}
        io_war = {}
        ps_war = {}
        x1_war = {}
        h_war = {}
        prev_mm = 0
        for pair in range(4):
            t0 = pair * 2 * SB
            for hf in range(2):
                h_war[hf] = h_war.get(hf, [])
                p.dma("sync", sb3(hres[hf][:, :], FC),
                      fm3(cx.dram["h0"], t0 + hf * SB, t0 + (hf + 1) * SB),
                      waits=merge_w(h_war[hf], [("t", prev_mm)]))
            for m in range(DC):
                par = m % 2
                ldw = p.dma("gpsimd", sb3(w1l[par][:, :], FC),
                            cx.dram["w1b"][:, :, m * 128:(m + 1) * 128]
                            .transpose([1, 0, 2]),
                            waits=merge_w(w_war.get(par, []), [("t", prev_mm)]))
                p.dma("sync", xf[par][:, :],
                      cx.dram["x_fm"][m, :, t0:t0 + 2 * SB],
                      waits=io_war.get(par, []))
                p.dma("sync", of[par][:, :],
                      cx.dram["o0"][m, :, t0:t0 + 2 * SB])
                prev_mm = p.n["t"] + 1
                gxo = p.op("gpsimd", "tensor_tensor", xo[par][:, :], xf[par][:, :],
                           of[par][:, :], ALU.add,
                           waits=merge_w([("dS", None)], x1_war.get(par, []),
                                         [("g", p.n["g"])]),
                           inc="g")
                for hf in range(2):
                    pb = hf
                    psl = cx.ps_S[:, pb * 512:pb * 512 + 512]
                    for c in range(FC):
                        w0 = []
                        if c == 0:
                            w0 = [("dG", None), ("dS", None),
                                  ("v", ps_war.get(pb, 0))]
                        p.op("tensor", "matmul", psl,
                             w1l[par][:, c * 128:(c + 1) * 128],
                             hres[hf][:, c * SB:(c + 1) * SB],
                             waits=w0, start=(c == 0), stop=(c == FC - 1),
                             inc=("t" if c == FC - 1 else None))
                    tmm = p.n["t"]
                    vev = p.op("vector", "scalar_tensor_tensor",
                               x1t[par][:, hf * SB:(hf + 1) * SB], psl,
                               b1t[:, m:m + 1], xo[par][:, hf * SB:(hf + 1) * SB],
                               ALU.add, ALU.add,
                               waits=[("t", tmm), ("g", gxo)], inc="v")
                    ps_war[pb] = vev
                p.dma("sync", cx.dram["x1"][m, :, t0:t0 + 2 * SB], x1t[par][:, :],
                      waits=[("v", vev)])
                x1_war[par] = [("v", vev)]
                io_war[par] = [("g", gxo), ("dS", None)]
                w_war[par] = [("t", p.n["t"])]
            h_war[0] = h_war[1] = [("t", p.n["t"])]
        p.barrier()


def emit_D(cx):
    emit_A0(cx, src_name="x1", dst_name="xn1")
    _emit_proj_featmajor(cx, "q1p", "wq1T", "bq1", "q1", NH, xn_name="xn1")
    _emit_proj_featmajor(cx, "k1p", "wk1T", "bk1", "k1", KV1, xn_name="xn1")
    _emit_proj_featmajor(cx, "v1p", "wv1T", "bv1", "v1", KV1, xn_name="xn1")
    _emit_proj_featmajor(cx, "mbp", "wmbT", "bmb", "mb1", NH, epilogue="sigmoid",
                         xn_name="xn1")
    _emit_proj_featmajor(cx, "gp", "wgT", "bg", "g1", DC, epilogue="sigmoid",
                         xn_name="xn1")
    emit_router(cx)


def emit_router(cx):
    p, nc = cx.p, cx.nc
    with contextlib.ExitStack() as es:
        sb = lambda n, sh, dt: es.enter_context(nc.sbuf_tensor(n, sh, dt))
        xr = sb("rt_x", [128, DC * NT], BF16)
        wr = sb("rt_w", [128, DC * NE], BF16)
        lg = [sb(f"rt_l{i}", [8, 512], F32) for i in range(2)]
        ldx = p.dma("sync", sb3(xr[:, :], DC), fm3(cx.dram["xn1"], 0, NT))
        ldw = p.dma("gpsimd", sb3(wr[:, :], DC),
                    cx.dram["wrT"].rearrange("(c p) m -> p c m", p=128))
        ev_war = {}
        for i in range(NST):
            par = i % 2
            t0 = i * ST
            for c in range(DC):
                w0 = []
                if c == 0:
                    w0 = merge_w([("dG", ldw), ("dS", ldx)],
                                 [("v", ev_war.get("ps", 0))])
                p.op("tensor", "matmul", cx.ps_A[0:8, 0:512],
                     wr[:, c * NE:(c + 1) * NE],
                     xr[:, c * NT + t0:c * NT + t0 + ST],
                     waits=w0, start=(c == 0), stop=(c == DC - 1),
                     inc=("t" if c == DC - 1 else None))
            tmm = p.n["t"]
            vev = p.op("vector", "tensor_copy", lg[par][:, :], cx.ps_A[0:8, 0:512],
                       waits=merge_w([("t", tmm)], ev_war.get(par, [])), inc="v")
            ev_war["ps"] = vev
            stv = p.dma("sync", cx.dram["lg1"][:, t0:t0 + ST], lg[par][:, :],
                        waits=[("v", vev)])
            ev_war[par] = [("dS", None)]
        p.barrier()




def _dpfp_front(cx, p, name, src_head_ap, mk_tm, xx, idn=None, waits0=(),
                mk_war=0):
    ktm = xx["ktm"]
    xxa = xx["xx"]
    ev_prev = 0
    for g8 in range(8):
        jj = g8 % 2
        for k4 in range(4):
            st32 = g8 * 4 + k4
            p.op("tensor", "transpose",
                 cx.ps_BF[:, jj * 512 + k4 * 128:jj * 512 + (k4 + 1) * 128],
                 src_head_ap[:, st32 * 128:(st32 + 1) * 128], idn,
                 waits=(merge_w(list(waits0), [("v", ev_prev)])
                        if k4 == 0 else []),
                 inc=("t" if k4 == 3 else None))
        tv = p.n["t"]
        ev_prev = p.op("vector", "tensor_copy", ktm[:, g8 * 512:(g8 + 1) * 512],
                       cx.ps_BF[:, jj * 512:(jj + 1) * 512],
                       waits=[("t", tv)], inc="v")
    ktm3 = ktm[:, :].rearrange("p (s t) -> p s t", s=32)
    xx3 = xxa[:, :].rearrange("p (s t) -> p s t", s=32)
    mk3 = mk_tm[:, :].rearrange("p (s t) -> p s t", s=32)
    p.op("vector", "tensor_scalar_max", xx3[:, :, 0:128], ktm3, 0.0,
         chain=True, inc="v")
    vx = p.op("vector", "tensor_scalar", xx3[:, :, 128:256], ktm3, -1.0, 0.0,
              ALU.mult, ALU.max, chain=True, inc="v")
    j = 1
    p.op("vector", "tensor_tensor", mk3[:, :, 0:1],
         xx3[:, :, 0:1], xx3[:, :, 255:256], ALU.mult,
         waits=[("t", mk_war)], chain=True, inc="v")
    p.op("vector", "tensor_tensor", mk3[:, :, 1:256],
         xx3[:, :, 1:256], xx3[:, :, 0:255], ALU.mult,
         chain=True, inc="v")
    for gi, j in enumerate((2, 3)):
        off = (j - 1) * 256
        p.op("gpsimd", "tensor_tensor", mk3[:, :, off:off + j],
             xx3[:, :, 0:j], xx3[:, :, 256 - j:256], ALU.mult,
             waits=([("v", vx), ("t", mk_war), ("g", p.n["g"])]
                    if gi == 0 else []),
             chain=(gi > 0), inc="g")
        p.op("gpsimd", "tensor_tensor", mk3[:, :, off + j:off + 256],
             xx3[:, :, j:256], xx3[:, :, 0:256 - j], ALU.mult,
             chain=True, inc="g")
    return p.n["v"], p.n["g"]


def _dpfp_normalize(cx, p, mk_tm, trash, rows, want_mksq=False):
    sumsq, rstd = rows["sumsq"], rows["rstd"]
    for st in range(32):
        p.op("scalar", "activation", trash[:, 0:768],
             mk_tm[:, st * 768:(st + 1) * 768], AF.Square,
             accum_out=sumsq[:, st:st + 1],
             waits=([("v", p.n["v"]), ("g", p.n["g"])] if st == 0 else []),
             chain=True, inc="s")
    sq_all = p.n["s"]
    rv = p.op("vector", "reciprocal", rstd[:, :], sumsq[:, :],
              waits=[("s", sq_all)], inc="v")
    rs = p.op("scalar", "activation", rstd[:, :], rstd[:, :], AF.Sqrt,
              waits=[("v", rv)], chain=True, inc="s")
    for st in range(32):
        mk_sl = mk_tm[:, st * 768:(st + 1) * 768]
        nv = p.op("vector", "tensor_scalar_mul", mk_sl, mk_sl,
                  rstd[:, st:st + 1],
                  waits=([("s", rs)] if st == 0 else []), chain=True, inc="v")
        if want_mksq:
            p.op("scalar", "activation", trash[:, 0:768], mk_sl, AF.Square,
                 accum_out=rows["mksq"][:, st:st + 1],
                 waits=[("v", nv)], chain=True, inc="s")


def _dpfp_ft(cx, p, mk_tm, mk_ft, idn=None, pre_waits=()):
    mk_ft3 = sb3(mk_ft[:, :], FC1)
    ev2 = 0
    dn_g = p.n["g"]
    dn_v = p.n["v"]
    for st in range(32):
        for c in range(FC1):
            w0 = []
            if c == 0:
                evw = [ev2] if isinstance(ev2, tuple) else []
                w0 = merge_w([("v", dn_v), ("g", dn_g)] if st == 0 else [],
                             evw, list(pre_waits) if st == 0 else [])
            p.op("tensor", "transpose",
                 cx.ps_BF[:, c * 128:(c + 1) * 128],
                 mk_tm[:, st * 768 + c * 128:st * 768 + (c + 1) * 128], idn,
                 waits=w0, inc=("t" if c == FC1 - 1 else None))
        tv = p.n["t"]
        eng = "vector" if st % 2 == 0 else "scalar"
        fname = "tensor_copy" if eng == "vector" else "copy"
        ev2 = p.op(eng, fname,
                   mk_ft3[:, :, st * 128:(st + 1) * 128],
                   cx.ps_BF[:, 0:768].rearrange("p (c t) -> p c t", c=FC1),
                   waits=[("t", tv)], chain=True, inc=ENG_SEM[eng])
        ev2 = (ENG_SEM[eng], ev2)
    return p.n["v"], p.n["t"]


def _emit_dpfp(cx, p, es, name, src_head_ap, mk_tm, mk_ft, xx, trash, rows,
               want_mksq=False, idn=None, waits0=()):
    _dpfp_front(cx, p, name, src_head_ap, mk_tm, xx, idn=idn, waits0=waits0)
    _dpfp_normalize(cx, p, mk_tm, trash, rows, want_mksq=want_mksq)
    _dpfp_ft(cx, p, mk_tm, mk_ft, idn=idn)
    return p.n["v"], p.n["t"]


def emit_E(cx):
    p, nc = cx.p, cx.nc
    with contextlib.ExitStack() as es:
        sb = lambda n, sh, dt: es.enter_context(nc.sbuf_tensor(n, sh, dt))
        ksb = sb("e_k", [128, NT], BF16)
        ktm = sb("e_ktm", [128, NT], BF16)
        x2 = sb("e_xx", [128, 32 * 256], BF16)
        mk_tm = sb("e_mktm", [128, 32 * FD1], BF16)
        mk_ft = sb("e_mkft", [128, FC1 * NT], BF16)
        trash = sb("e_trash", [128, FD1], BF16)
        sumsq = sb("e_ssq", [128, 32], F32)
        rstd = sb("e_rstd", [128, 32], F32)
        mksq = sb("e_msq", [128, 32], F32)
        rkq = sb("e_rkq", [128, 32], F32)
        mem_sb = sb("e_mem", [128, FC1 * 128], BF16)
        normall = sb("e_norm", [128, FC1 * 16], BF16)
        mbs = sb("e_mb", [128, NT], BF16)
        vsb = sb("e_v", [128, NT], BF16)
        denr = [sb(f"e_denr{i}", [1, 512], F32) for i in range(2)]
        rrowf = [sb(f"e_rrowf{i}", [1, 512], F32) for i in range(2)]
        rrow = [sb(f"e_rrow{i}", [1, 512], BF16) for i in range(2)]
        bcr = [sb(f"e_bcr{i}", [128, 512], BF16) for i in range(2)]
        pvt = [sb(f"e_pv{i}", [128, 512], BF16) for i in range(2)]
        mvt = [sb(f"e_mv{i}", [128, 512], BF16) for i in range(2)]
        wmv = [sb(f"e_wmv{i}", [128, 512], BF16) for i in range(2)]
        wmv_tm = sb("e_wmvtm", [128, 512], BF16)
        dent = sb("e_dent", [128, 4], F32)
        ctmp = sb("e_ctmp", [128, 4], F32)
        coef = sb("e_coef", [128, 4], BF16)
        dev_t = sb("e_dev", [128, FD1], F32)
        dnr_acc = sb("e_dnr", [1, FD1], F32)
        idn = sb("e_idn", [128, 128], BF16)
        ones_row = sb("e_ones", [1, 128], BF16)
        one1f = sb("e_one1", [1, 1], F32)

        p.op("vector", "memset", ones_row[:, :], 1.0, inc="v")
        p.op("vector", "memset", one1f[:, :], 1.0, inc="v")
        p.dma("gpsimd", idn[:, :], cx.dram["ident"][:, :])
        p.dma("gpsimd", sb3(normall[:, :], FC1),
              cx.dram["norm0T"].rearrange("(c p) h -> p c h", p=128))

        prev_delta_ev = 0
        slot_war = {}
        bc_ps_e = [cx.ps_C[:, 0:512], cx.ps_S[:, 1024:1536]]
        import os as _os
        _EPART = _os.environ.get("E_PART", "all")
        for kh in range(KV1):
            ldk = p.dma("sync", ksb[:, :], cx.dram["k1"][kh, :, :],
                        waits=[("t", p.n["t"])])
            ldv = p.dma("sync", vsb[:, :], cx.dram["v1"][kh, :, :],
                        waits=[("g", p.n["g"])])
            vd, td = _emit_dpfp(cx, p, es, f"e{kh}", ksb, mk_tm, mk_ft,
                                {"ktm": ktm, "xx": x2}, trash,
                                {"sumsq": sumsq, "rstd": rstd, "mksq": mksq},
                                want_mksq=True, idn=idn[:, :],
                                waits0=[("dS", None), ("dG", None)])
            rkv = p.op("vector", "reciprocal", rkq[:, :], mksq[:, :],
                       waits=[("s", p.n["s"])], inc="v")
            if _EPART == "dpfp":
                continue
            for qhi in range(4):
                h = kh * 4 + qhi
                ldm = p.dma("gpsimd", sb3(mem_sb[:, :], FC1),
                            cx.dram["mem0"][h].rearrange("(c p) d -> p c d", p=128),
                            waits=[("t", p.n["t"])])
                ldmb = p.dma("sync", mbs[:, :], cx.dram["mb1"][h, :, :],
                             waits=[("g", p.n["g"]), ("t", p.n["t"])])
                dnacc_v = p.op("vector", "memset", dnr_acc[0:1, :], 0.0,
                               waits=[("dS", None)], chain=True, inc="v")
                first_delta = True
                if _EPART == "heads":
                    continue
                for st8 in range(NST):
                    t0 = st8 * ST
                    sl = st8 % 2
                    num_ps = cx.ps_A[:, 0:512] if sl == 0 else cx.ps_S[:, 1536:2048]
                    for c in range(FC1):
                        w0 = []
                        if c == 0:
                            w0 = [("dG", None), ("dS", None),
                                  ("v", slot_war.get(("n", sl), 0))]
                        p.op("tensor", "matmul", num_ps,
                             mem_sb[:, c * 128:(c + 1) * 128],
                             mk_ft[:, c * NT + t0:c * NT + t0 + ST],
                             waits=w0, start=(c == 0), stop=(c == FC1 - 1),
                             inc=("t" if c == FC1 - 1 else None))
                    tnum = p.n["t"]
                    for c in range(FC1):
                        p.op("tensor", "matmul", cx.ps_B[0:1, 0:512],
                             normall[:, c * 16 + h:c * 16 + h + 1],
                             mk_ft[:, c * NT + t0:c * NT + t0 + ST],
                             waits=([("v", slot_war.get("d", 0))] if c == 0 else []),
                             start=(c == 0), stop=(c == FC1 - 1),
                             inc=("t" if c == FC1 - 1 else None))
                    tden = p.n["t"]
                    vde = p.op("vector", "tensor_scalar_add", denr[sl][0:1, :],
                               cx.ps_B[0:1, 0:512], EPS1,
                               waits=[("t", tden)], inc="v")
                    p.op("vector", "reciprocal", rrowf[sl][0:1, :],
                         denr[sl][0:1, :], chain=True, inc="v")
                    vrr = p.op("vector", "tensor_copy", rrow[sl][0:1, :],
                               rrowf[sl][0:1, :], chain=True, inc="v")
                    tbc = p.op("tensor", "matmul", bc_ps_e[sl], ones_row[0:1, :],
                               rrow[sl][0:1, :],
                               waits=merge_w([("v", vrr)],
                                             [("v", slot_war.get(("b", sl), 0))]),
                               start=True, stop=True, inc="t")
                    vbc = p.op("vector", "tensor_copy", bcr[sl][:, :], bc_ps_e[sl],
                               waits=[("t", tbc)], chain=True, inc="v")
                    slot_war[("b", sl)] = vbc
                    vpv = p.op("vector", "scalar_tensor_tensor", pvt[sl][:, :],
                               num_ps, 1.0, bcr[sl][:, :],
                               ALU.mult, ALU.mult, chain=True, inc="v")
                    slot_war[("n", sl)] = vpv
                    gmv = p.op("gpsimd", "tensor_tensor", mvt[sl][:, :],
                               vsb[:, t0:t0 + ST], pvt[sl][:, :], ALU.subtract,
                               waits=[("v", vpv), ("dS", None), ("g", p.n["g"])],
                               inc="g")
                    gwm = p.op("gpsimd", "tensor_tensor", wmv[sl][:, :],
                               mvt[sl][:, :], mbs[:, t0:t0 + ST], ALU.mult,
                               chain=True, inc="g")
                    for j4 in range(4):
                        p.op("tensor", "matmul", cx.ps_B[:, 4 + j4:5 + j4],
                             denr[sl][0:1, j4 * 128:(j4 + 1) * 128],
                             one1f[0:1, 0:1],
                             waits=([("v", vrr)] if j4 == 0 else []),
                             start=True, stop=True,
                             inc=("t" if j4 == 3 else None))
                    tdtm = p.n["t"]
                    vdt = p.op("vector", "tensor_copy", dent[:, :],
                               cx.ps_B[:, 4:8], waits=[("t", tdtm)], inc="v")
                    slot_war["d"] = vdt
                    vc1 = p.op("vector", "tensor_tensor", ctmp[:, :], dent[:, :],
                               rkq[:, st8 * 4:st8 * 4 + 4], ALU.mult,
                               chain=True, inc="v")
                    vc2 = p.op("vector", "tensor_scalar", coef[:, :], ctmp[:, :],
                               -1.0, 1.0, ALU.mult, ALU.add, chain=True, inc="v")
                    vc3 = p.op("vector", "tensor_scalar", coef[:, :], coef[:, :],
                               0.0, 1.0, ALU.max, ALU.min, chain=True, inc="v")
                    if _EPART == "numden":
                        continue
                    for k32 in range(4):
                        st32 = st8 * 4 + k32
                        last_d = (st8 == NST - 1 and k32 == 3)
                        ttw = p.op("tensor", "transpose", cx.ps_BF[:, 0:128],
                                   wmv[sl][:, k32 * 128:(k32 + 1) * 128], idn[:, :],
                                   waits=[("g", gwm), ("v", p.n["v"])], inc="t")
                        vtw = p.op("vector", "tensor_copy", wmv_tm[:, 0:128],
                                   cx.ps_BF[:, 0:128], waits=[("t", ttw)], inc="v")
                        p.op("tensor", "matmul", cx.ps_S[:, 0:512],
                             wmv_tm[:, 0:128],
                             mk_tm[:, st32 * 768:st32 * 768 + 512],
                             waits=merge_w([("v", vtw)],
                                           [("v", prev_delta_ev)]
                                           if first_delta else []),
                             start=first_delta, stop=last_d)
                        p.op("tensor", "matmul", cx.ps_S[:, 512:768],
                             wmv_tm[:, 0:128],
                             mk_tm[:, st32 * 768 + 512:st32 * 768 + 768],
                             start=first_delta, stop=last_d,
                             inc=("t" if last_d else None))
                        dn_a = bc_ps_e[sl][0:1, 0:512]
                        dn_b = bc_ps_e[1 - sl][0:1, 0:256]
                        p.op("tensor", "matmul", dn_a,
                             coef[:, k32:k32 + 1],
                             mk_tm[:, st32 * 768:st32 * 768 + 512],
                             waits=(merge_w([("v", vc3)], [("v", dnacc_v)])
                                    if k32 == 0 else []),
                             start=(k32 == 0), stop=(k32 == 3))
                        p.op("tensor", "matmul", dn_b,
                             coef[:, k32:k32 + 1],
                             mk_tm[:, st32 * 768 + 512:st32 * 768 + 768],
                             start=(k32 == 0), stop=(k32 == 3),
                             inc=("t" if k32 == 3 else None))
                        first_delta = False
                    tdn = p.n["t"]
                    p.op("vector", "tensor_tensor", dnr_acc[0:1, 0:512],
                         dnr_acc[0:1, 0:512], dn_a, ALU.add,
                         waits=[("t", tdn)], chain=True, inc="v")
                    dnacc_v = p.op("vector", "tensor_tensor", dnr_acc[0:1, 512:768],
                                   dnr_acc[0:1, 512:768], dn_b,
                                   ALU.add, chain=True, inc="v")
                    slot_war[("b", sl)] = dnacc_v
                    slot_war[("b", 1 - sl)] = dnacc_v
                if _EPART in ("heads", "numden"):
                    continue
                tdel = p.n["t"]
                ve1 = p.op("vector", "tensor_copy", dev_t[:, :],
                           cx.ps_S[:, 0:768], waits=[("t", tdel)], inc="v")
                p.dma("sync", cx.dram["ccmi"][:, h * FD1:(h + 1) * FD1],
                      dev_t[:, :], waits=[("v", ve1)])
                p.dma("sync",
                      cx.dram["ccni"].rearrange("(o f) -> o f", o=1)[0:1, h * FD1:(h + 1) * FD1],
                      dnr_acc[0:1, :], waits=[("v", dnacc_v)])
                prev_delta_ev = ve1
        p.barrier()
        rg = [list(range(COLL_CORES))]
        p.op("gpsimd", "collective_compute", "AllReduce", ALU.add, rg,
             [cx.dram["ccmi"][:, :]], [cx.dram["ccmo"][:, :]],
             waits=[("dS", None)], inc="cc", incby=1)
        p.op("gpsimd", "collective_compute", "AllReduce", ALU.add, rg,
             [cx.dram["ccni"].rearrange("(o f) -> o f", o=1)],
             [cx.dram["ccno"].rearrange("(o f) -> o f", o=1)],
             inc="cc", incby=1)
        p.barrier()
    with contextlib.ExitStack() as es2:
        sb = lambda n, sh, dt: es2.enter_context(nc.sbuf_tensor(n, sh, dt))
        dmr = sb("e2_dmr", [128, NH * FD1], F32)
        memall = sb("e2_mem", [128, NH * FC1 * 128], BF16)
        nm_sb = sb("e2_nm", [128, NH * FC1 * 128], BF16)
        dnrow = sb("e2_dnr", [1, NH * FD1], F32)
        normf = sb("e2_nf", [128, NH * FC1], F32)
        nnr_sb = sb("e2_nnr", [128, NH * FC1], BF16)
        idnf2 = sb("e2_idnf", [128, 128], F32)
        one1f2 = sb("e2_one1", [1, 1], F32)
        p.op("vector", "memset", one1f2[:, :], 1.0, inc="v")
        ld1 = p.dma("sync", dmr[:, :], cx.dram["ccmo"][:, :])
        ld2 = p.dma("sync", dnrow[0:1, :],
                    cx.dram["ccno"].rearrange("(o f) -> o f", o=1))
        ld3 = p.dma("gpsimd", memall[:, :].rearrange("p (h c d) -> p h c d", h=NH, c=FC1),
                    cx.dram["mem0"].rearrange("h (c p) d -> p h c d", p=128))
        ld4 = p.dma("sync", idnf2[:, :], cx.dram["ident"][:, :])
        ld5 = p.dma("gpsimd", sb3(normf[:, :], FC1),
                    cx.dram["norm0T"].rearrange("(c p) h -> p c h", p=128))
        for c in range(FC1):
            for h in range(NH):
                j = c * 16 + h
                p.op("tensor", "matmul", cx.ps_B[:, j:j + 1],
                     dnrow[0:1, h * FD1 + c * 128:h * FD1 + (c + 1) * 128],
                     one1f2[0:1, 0:1],
                     waits=([("dS", None), ("v", p.n["v"])] if j == 0 else []),
                     start=True, stop=True,
                     inc=("t" if (c == FC1 - 1 and h == NH - 1) else None))
        tnr = p.n["t"]
        vnr = p.op("vector", "tensor_tensor", nnr_sb[:, :], cx.ps_B[:, 0:96],
                   normf[:, :], ALU.add, waits=[("t", tnr), ("dG", None)], inc="v")
        p.dma("sync", cx.dram["nnr"][:, :], nnr_sb[:, :], waits=[("v", vnr)])
        vprev = 0
        for h in range(NH):
            for c in range(FC1):
                jj = (h * FC1 + c) % 2
                tt = p.op("tensor", "transpose",
                          cx.ps_C[:, jj * 128:(jj + 1) * 128],
                          dmr[:, h * FD1 + c * 128:h * FD1 + (c + 1) * 128],
                          idnf2[:, :],
                          waits=merge_w([("dS", None)] if (h == 0 and c == 0) else [],
                                        [("v", vprev)]),
                          inc="t")
                off = (h * FC1 + c) * 128
                vprev = p.op("vector", "tensor_tensor", nm_sb[:, off:off + 128],
                             cx.ps_C[:, jj * 128:(jj + 1) * 128],
                             memall[:, off:off + 128], ALU.add,
                             waits=[("t", tt), ("dG", None)], inc="v")
        p.dma("sync", cx.dram["nmem"][:, :], nm_sb[:, :], waits=[("v", vprev)])
        p.barrier()


def emit_F(cx):
    p, nc = cx.p, cx.nc
    with contextlib.ExitStack() as es:
        sb = lambda n, sh, dt: es.enter_context(nc.sbuf_tensor(n, sh, dt))
        qsb0 = sb("f_q0", [128, NT], BF16)
        qsb = [qsb0, qsb0]
        qtm = sb("f_qtm", [128, NT], BF16)
        x2 = sb("f_xx", [128, 32 * 256], BF16)
        mq_tm = sb("f_mqtm", [128, 32 * FD1], BF16)
        mq_ft = [sb(f"f_mqft{i}", [128, FC1 * NT], BF16) for i in range(2)]
        nm_h = [sb(f"f_nm{i}", [128, FC1 * 128], BF16) for i in range(2)]
        nnr_sb = sb("f_nnr", [128, NH * FC1], BF16)
        denall = sb("f_denall", [1, NT], F32)
        mh_st = [sb(f"f_mh{i}", [128, 512], BF16) for i in range(2)]
        idn = sb("f_idn", [128, 128], BF16)
        p.dma("gpsimd", idn[:, :], cx.dram["ident"][:, :])
        p.dma("sync", nnr_sb[:, :], cx.dram["nnr"][:, :])
        slot_war = {}
        ft_war = {0: 0, 1: 0}
        mk_war = 0
        q_war = {0: [], 1: []}
        prev_first_mm = 0
        for qh in range(NH + 1):
            if qh < NH:
                par = qh % 2
                ldq = p.dma("sync", qsb[par][:, :], cx.dram["q1"][qh, :, :],
                            waits=merge_w(q_war[par], [("t", prev_first_mm)]))
                prev_first_mm = p.n["t"] + 1
                _dpfp_front(cx, p, f"f{qh}", qsb[par], mq_tm,
                            {"ktm": qtm, "xx": x2}, idn=idn[:, :],
                            waits0=[("dS", None), ("dG", None), ("v", None),
                                    ("s", None)],
                            mk_war=mk_war)
                q_war[0] = q_war[1] = [("t", p.n["t"])]
            if qh > 0:
                qp = qh - 1
                slp = qp % 2
                ft = mq_ft[slp]
                p.dma("gpsimd", nm_h[slp][:, :],
                      cx.dram["nmem"][:, qp * FD1:(qp + 1) * FD1],
                      waits=[("t", None)])
                denall_v = 0
                for st8 in range(NST):
                    t0 = st8 * ST
                    sl = st8 % 2
                    num_ps = cx.ps_A[:, 0:512] if sl == 0 else cx.ps_S[:, 0:512]
                    den_ps = (cx.ps_B[0:1, 0:512] if sl == 0
                              else cx.ps_S[0:1, 512:1024])
                    for c in range(FC1):
                        w0 = []
                        if c == 0:
                            w0 = [("dS", None), ("dG", None),
                                  ("s", slot_war.get(("n", sl), 0))]
                        p.op("tensor", "matmul", num_ps,
                             nm_h[slp][:, c * 128:(c + 1) * 128],
                             ft[:, c * NT + t0:c * NT + t0 + ST],
                             waits=w0, start=(c == 0), stop=(c == FC1 - 1),
                             inc=("t" if c == FC1 - 1 else None))
                    tnum = p.n["t"]
                    for c in range(FC1):
                        p.op("tensor", "matmul", den_ps,
                             nnr_sb[:, c * 16 + qp:c * 16 + qp + 1],
                             ft[:, c * NT + t0:c * NT + t0 + ST],
                             waits=([("v", slot_war.get(("d", sl), 0))]
                                    if c == 0 else []),
                             start=(c == 0), stop=(c == FC1 - 1),
                             inc=("t" if c == FC1 - 1 else None))
                    tden = p.n["t"]
                    smh = p.op("scalar", "copy", mh_st[sl][:, :], num_ps,
                               waits=merge_w([("t", tnum)],
                                             [("dS", None)]
                                             if slot_war.get(("m", sl), 0) else []),
                               chain=True, inc="s")
                    slot_war[("n", sl)] = smh
                    stv = p.dma("sync", cx.dram["mh1"][qp, :, t0:t0 + ST],
                                mh_st[sl][:, :], waits=[("s", smh)])
                    slot_war[("m", sl)] = stv
                    denall_v = p.op("vector", "tensor_scalar_add",
                                    denall[0:1, t0:t0 + ST], den_ps, EPS1,
                                    waits=[("t", tden)], chain=True, inc="v")
                    slot_war[("d", sl)] = denall_v
                p.dma("sync",
                      cx.dram["dh1"].rearrange("(o h) t -> o (h t)", o=1)[0:1, qp * NT:(qp + 1) * NT],
                      denall[0:1, :], waits=[("v", denall_v)])
                ft_war[slp] = p.n["t"]
                slot_war[("nm", slp)] = p.n["t"]
            if qh < NH:
                _, tft = _dpfp_ft(cx, p, mq_tm, mq_ft[qh % 2], idn=idn[:, :],
                                  pre_waits=[("t", ft_war[qh % 2])])
                mk_war = tft
        p.barrier()


def emit_G(cx):
    p, nc = cx.p, cx.nc
    BIG = 1.0e30
    with contextlib.ExitStack() as es:
        sb = lambda n, sh, dt: es.enter_context(nc.sbuf_tensor(n, sh, dt))
        xr = sb("g_x", [128, DC * NT], BF16)
        w_fm = sb("g_wfm", [8, NT], BF16)
        Hs_sb = sb("g_Hs", [64, NT], BF16)
        wbc = sb("g_wbc", [64, D], BF16)
        bbt = sb("g_bb", [128, DC], F32)
        p.dma("sync", sb3(xr[:, :], DC), fm3(cx.dram["xn1"], 0, NT))
        p.dma("sync", bbt[:, :], cx.dram["bb"][:, :])
        p.dma("gpsimd", wbc[:, :], cx.dram["wbcT"][:, :])

        with contextlib.ExitStack() as es1:
            sb1 = lambda n, sh, dt: es1.enter_context(nc.sbuf_tensor(n, sh, dt))
            lgs = sb1("g_lg", [8, NT], F32)
            lgtm = sb1("g_lgtm", [128, 8], F32)
            m1t = sb1("g_m1", [128, 1], F32)
            nm1t = sb1("g_nm1", [128, 1], F32)
            mask1 = sb1("g_msk1", [128, 8], F32)
            t1t = sb1("g_t1", [128, 8], F32)
            tmpt = sb1("g_tmp", [128, 8], F32)
            m2t = sb1("g_m2", [128, 1], F32)
            mask2 = sb1("g_msk2", [128, 8], F32)
            pexp = sb1("g_pexp", [128, 8], F32)
            e2t = sb1("g_e2", [128, 1], F32)
            d1t = sb1("g_d1", [128, 1], F32)
            rt = sb1("g_r", [128, 1], F32)
            wt1 = sb1("g_wt1", [128, 8], F32)
            w_tm = sb1("g_wtm", [128, 32 * 8], BF16)
            idn = sb1("g_idn", [128, 128], BF16)
            idnf = sb1("g_idnf", [128, 128], F32)
            p.dma("sync", lgs[:, :], cx.dram["lg1"][:, :])
            p.dma("sync", idnf[:, :], cx.dram["ident"][:, :])
            p.dma("gpsimd", idn[:, :], cx.dram["ident"][:, :])
            vpw = 0
            for st in range(32):
                tlg = p.op("tensor", "transpose", cx.ps_C[:, 0:8],
                           lgs[0:8, st * 128:(st + 1) * 128], idnf[0:8, 0:8],
                           waits=merge_w([("dS", None), ("dG", None)]
                                         if st == 0 else [], [("v", vpw)]),
                           inc="t")
                vl = p.op("vector", "tensor_copy", lgtm[:, :], cx.ps_C[:, 0:8],
                          waits=[("t", tlg)], inc="v")
                vm1 = p.op("vector", "reduce_max", m1t[:, 0:1], lgtm[:, :],
                           axis=AX, chain=True, inc="v")
                vnm1 = p.op("vector", "reduce_max", nm1t[:, 0:1], lgtm[:, :],
                            axis=AX, negate=True, chain=True, inc="v")
                vmask1 = p.op("vector", "tensor_single_scalar", mask1[:, :],
                              lgtm[:, :], m1t[:, 0:1], ALU.is_ge, chain=True,
                              inc="v")
                gt1 = p.op("gpsimd", "tensor_scalar_mul", t1t[:, :], mask1[:, :],
                           -BIG, waits=[("v", vmask1), ("g", p.n["g"])], inc="g")
                gtmp = p.op("gpsimd", "tensor_tensor", tmpt[:, :], lgtm[:, :],
                            t1t[:, :], ALU.add, chain=True, inc="g")
                vm2 = p.op("vector", "reduce_max", m2t[:, 0:1], tmpt[:, :],
                           axis=AX, waits=[("g", gtmp)], inc="v")
                vmask2 = p.op("vector", "tensor_single_scalar", mask2[:, :],
                              lgtm[:, :], m2t[:, 0:1], ALU.is_ge, chain=True,
                              inc="v")
                sp = p.op("scalar", "activation", pexp[:, :], lgtm[:, :], AF.Exp,
                          bias=nm1t[:, 0:1], waits=[("v", vmask2), ("s", p.n["s"])],
                          inc="s")
                se2 = p.op("scalar", "activation", e2t[:, 0:1], m2t[:, 0:1],
                           AF.Exp, bias=nm1t[:, 0:1], chain=True, inc="s")
                vd1 = p.op("vector", "tensor_scalar_add", d1t[:, 0:1],
                           e2t[:, 0:1], 1.0, waits=[("s", se2)], inc="v")
                vr = p.op("vector", "reciprocal", rt[:, 0:1], d1t[:, 0:1],
                          chain=True, inc="v")
                vw1 = p.op("vector", "tensor_tensor", wt1[:, :], pexp[:, :],
                           mask2[:, :], ALU.mult, waits=[("s", sp)], chain=True,
                           inc="v")
                vw2 = p.op("vector", "tensor_scalar_mul",
                           w_tm[:, st * 8:(st + 1) * 8], wt1[:, :], rt[:, 0:1],
                           chain=True, inc="v")
                tw = p.op("tensor", "transpose", cx.ps_BF[0:8, 0:128],
                          w_tm[:, st * 8:(st + 1) * 8], idn[:, :],
                          waits=[("v", vw2)], inc="t")
                vpw = p.op("vector", "tensor_copy",
                           w_fm[0:8, st * 128:(st + 1) * 128],
                           cx.ps_BF[0:8, 0:128], waits=[("t", tw)], inc="v")
            wdone = vpw

        with contextlib.ExitStack() as es2:
            sb2 = lambda n, sh, dt: es2.enter_context(nc.sbuf_tensor(n, sh, dt))
            H_sb = sb2("g_H", [64, NT], BF16)
            wa = sb2("g_wa", [128, DC * 64], BF16)
            oh = sb2("g_oh", [8, NE * NR], BF16)
            ldo = p.dma("gpsimd", oh[:, :], cx.dram["oh864"][:, :],
                        waits=[("t", p.n["t"])])
            ldwa = p.dma("gpsimd", sb3(wa[:, :], DC),
                         cx.dram["waT"].rearrange("(c p) m -> p c m", p=128))
            hprev = 0
            for i in range(NST):
                t0 = i * ST
                for c in range(DC):
                    w0 = []
                    if c == 0:
                        w0 = [("dG", None), ("dS", None), ("v", hprev),
                              ("v", wdone)]
                    p.op("tensor", "matmul", cx.ps_A[0:64, 0:512],
                         wa[:, c * 64:(c + 1) * 64],
                         xr[:, c * NT + t0:c * NT + t0 + ST],
                         waits=w0, start=(c == 0), stop=(c == DC - 1),
                         inc=("t" if c == DC - 1 else None))
                th = p.n["t"]
                hprev = p.op("vector", "tensor_copy", H_sb[0:64, t0:t0 + ST],
                             cx.ps_A[0:64, 0:512], waits=[("t", th)], inc="v")
            hsprev = 0
            for i in range(NST):
                t0 = i * ST
                tb = p.op("tensor", "matmul", cx.ps_C[0:64, 0:512], oh[0:8, :],
                          w_fm[0:8, t0:t0 + ST],
                          waits=merge_w([("v", wdone), ("dG", None)]
                                        if i == 0 else [], [("v", hsprev)]),
                          start=True, stop=True, inc="t")
                hsprev = p.op("vector", "scalar_tensor_tensor",
                              Hs_sb[0:64, t0:t0 + ST], cx.ps_C[0:64, 0:512],
                              MOE_SCALE, H_sb[0:64, t0:t0 + ST],
                              ALU.mult, ALU.mult,
                              waits=[("t", tb), ("v", hprev)], chain=True,
                              inc="v")
            hs_done = hsprev

        HB = NT // 2
        wb_ = [sb(f"g_wb{i}", [128, DC * 128], BF16) for i in range(2)]
        x1l = [sb(f"g_x1{i}", [128, HB], F32) for i in range(2)]
        gl0 = sb("g_g0", [128, HB], BF16)
        gl = [gl0, gl0]
        mhl0 = sb("g_mh0", [128, HB], BF16)
        mhl = [mhl0, mhl0]
        asm = [sb(f"g_asm{i}", [128, HB], F32) for i in range(2)]
        dpc0 = sb("g_dpc0", [16, 512], F32)
        dpc = [dpc0, dpc0]
        drb0 = sb("g_drb0", [1, HB], BF16)
        drb = [drb0, drb0]
        rbc0 = sb("g_rbc0", [128, HB], BF16)
        rbc = [rbc0, rbc0]
        onbf = sb("g_onbf", [1, 128], BF16)
        p.op("vector", "memset", onbf[:, :], 1.0, inc="v")
        dh1f = cx.dram["dh1"].rearrange("(o h) t -> o h t", o=1)
        drcf = cx.dram["drcp"].rearrange("(o h) t -> o h t", o=1)
        dprev = 0
        for i8 in range(NST):
            dp = i8 % 2
            ldd = p.dma("sync", dpc[dp][:, :],
                        dh1f[0, :, i8 * 512:(i8 + 1) * 512],
                        waits=merge_w([("v", dprev)],
                                      [("dS", None)] if i8 > 0 else [],
                                      [("v", wdone)] if i8 == 0 else []))
            dprev = p.op("vector", "reciprocal", dpc[dp][:, :], dpc[dp][:, :],
                         waits=[("dS", ldd)], chain=True, inc="v")
            p.dma("sync", drcf[0, :, i8 * 512:(i8 + 1) * 512], dpc[dp][:, :],
                  waits=[("v", dprev)])
        drcp_done = p.n["dS"]
        drw_war = 0
        gmh_war = 0
        w_war = {}
        io_war = {}
        ps_war = {}
        prev_mm = 0
        for m in range(DC):
            par = m % 2
            ldw = p.dma("gpsimd", sb3(wb_[par][:, :], DC),
                        cx.dram["wbT"][:, m * 128:(m + 1) * 128]
                        .rearrange("(c p) m -> p c m", p=128),
                        waits=merge_w(w_war.get(par, []), [("t", prev_mm)],
                                      [("v", hs_done)] if m < 2 else []))
            prev_mm0 = prev_mm
            for half in range(2):
                hp = half
                th0 = half * HB
                ldx1 = p.dma("sync", x1l[hp][:, :],
                             cx.dram["x1"][m, :, th0:th0 + HB],
                             waits=merge_w(io_war.get(hp, []),
                                           [("v", hs_done), ("t", prev_mm0),
                                            ("dS", None)]
                                           if m == 0 else []))
                p.dma("sync", gl[hp][:, :], cx.dram["g1"][m, :, th0:th0 + HB],
                      waits=[("g", gmh_war)])
                p.dma("sync", mhl[hp][:, :], cx.dram["mh1"][m, :, th0:th0 + HB],
                      waits=[("dS", None)])
                lddr = p.dma("gpsimd", drb[hp][0:1, :],
                             cx.dram["drcp"][m, th0:th0 + HB]
                             .rearrange("(o t) -> o t", o=1),
                             waits=merge_w([("dS", drcp_done)]
                                           if (m == 0 and half == 0) else [],
                                           [("t", drw_war)]))
                prev_mm = p.n["t"] + 1
                for ii in range(HB // ST):
                    bps = cx.ps_C[:, 0:512] if ii % 2 == 0 else cx.ps_A[:, 0:512]
                    tb2_ = p.op("tensor", "matmul", bps, onbf[0:1, :],
                                drb[hp][0:1, ii * ST:(ii + 1) * ST],
                                waits=merge_w([("dG", lddr)],
                                              [("v", ps_war.get(("r", ii % 2), 0))]),
                                start=True, stop=True, inc="t")
                    ps_war[("r", ii % 2)] = p.op(
                        "vector", "tensor_copy",
                        rbc[hp][:, ii * ST:(ii + 1) * ST], bps,
                        waits=[("t", tb2_)], chain=True, inc="v")
                rbc_done = p.n["v"]
                drw_war = p.n["t"]
                g1o = p.op("gpsimd", "tensor_tensor", asm[hp][:, :],
                           gl[hp][:, :], mhl[hp][:, :], ALU.mult,
                           waits=merge_w([("dS", None), ("g", p.n["g"])],
                                         io_war.get(("a", hp), [])),
                           inc="g")
                g1b = p.op("gpsimd", "tensor_tensor", asm[hp][:, :],
                           asm[hp][:, :], rbc[hp][:, :], ALU.mult,
                           waits=[("v", rbc_done)], chain=True, inc="g")
                gmh_war = g1o
                g2o = p.op("gpsimd", "tensor_tensor", asm[hp][:, :],
                           asm[hp][:, :], xr[:, m * NT + th0:m * NT + th0 + HB],
                           ALU.add, chain=True, inc="g")
                va = p.op("vector", "tensor_tensor", asm[hp][:, :],
                          asm[hp][:, :], x1l[hp][:, :], ALU.add,
                          waits=[("g", g2o), ("dS", None), ("v", p.n["v"])],
                          inc="v")
                vasm_last = 0
                for ii in range(HB // ST):
                    i = half * (HB // ST) + ii
                    t0 = i * ST
                    tl0 = ii * ST
                    pb = i % 2
                    psl = cx.ps_S[:, pb * 512:pb * 512 + 512]
                    for c in range(DC):
                        w0 = []
                        if c == 0:
                            w0 = [("dG", None), ("dS", None),
                                  ("v", ps_war.get(pb, 0))]
                        p.op("tensor", "matmul", psl,
                             wb_[par][:, c * 128:(c + 1) * 128],
                             xr[:, c * NT + t0:c * NT + t0 + ST],
                             waits=w0, start=(c == 0), stop=False)
                    p.op("tensor", "matmul", psl, wbc[:, m * 128:(m + 1) * 128],
                         Hs_sb[0:64, t0:t0 + ST],
                         waits=[("v", hs_done)], start=False, stop=True, inc="t")
                    tmm = p.n["t"]
                    v2 = p.op("vector", "scalar_tensor_tensor",
                              asm[hp][:, tl0:tl0 + ST], psl, bbt[:, m:m + 1],
                              asm[hp][:, tl0:tl0 + ST], ALU.add, ALU.add,
                              waits=merge_w([("t", tmm)], [("v", va)]),
                              chain=True, inc="v")
                    ps_war[pb] = v2
                    vasm_last = v2
                p.dma("sync", cx.dram["out_fm"][m, :, th0:th0 + HB],
                      asm[hp][:, :], waits=[("v", vasm_last)])
                io_war[hp] = [("v", vasm_last), ("dS", None)]
                io_war[("a", hp)] = [("dS", None)]
                io_war[("r", hp)] = [("g", g1b)]
            w_war[par] = [("t", p.n["t"])]
        p.barrier()



PHASES = {
    "A0": emit_A0,
    "A1a": emit_A1a,
    "A1b": emit_A1b,
    "B": emit_B,
    "C1": emit_C1,
    "C2": emit_C2,
    "D": emit_D,
    "E": emit_E,
    "F": emit_F,
    "G": emit_G,
}


def build_launch(phase_names, reads, writes):
    nc = bass.Bass("TRN2", target_bir_lowering=False)
    es = contextlib.ExitStack()
    dram = {}
    used_g = set()
    for ph in phase_names:
        used_g.update(PHASE_USES[ph])
    for name in sorted(used_g):
        shape, dt = GLOBALS_SHAPE[name]
        dram[name] = nc.dram_tensor(name, shape, dt, kind="ExternalInput").ap()
    for name, (shape, dt) in SCRATCH.items():
        if name in reads:
            kind = "ExternalInput"
        elif name in writes:
            kind = "ExternalOutput"
        else:
            kind = "Internal"
        dram[name] = nc.dram_tensor(name, shape, dt, kind=kind).ap()
    if "out_fm" in writes:
        dram["out_fm"] = nc.dram_tensor("out_fm", [DC, 128, NT], F32,
                                        kind="ExternalOutput").ap()
    cx = Ctx(nc, es, dram)
    for ph in phase_names:
        PHASES[ph](cx)
    cx.p.realize(nc, es)
    es.close()
    return nc, used_g


def run_phases(groups, inputs, core_ids=None):
    if core_ids is None:
        core_ids = list(range(NCORES))
    shared, per_core = prep_host(inputs)
    carry = [dict(pc) for pc in per_core]
    for (phs, reads, writes) in groups:
        nc, used_g = build_launch(phs, reads, writes)
        in_maps = []
        for c in range(len(core_ids)):
            m = {}
            for k in used_g:
                m[k] = carry[c][k] if k in carry[c] else shared[k]
            for k in reads:
                m[k] = carry[c][k]
            in_maps.append(m)
        import sys
        print(f"[run_phases] launching {phs}", file=sys.stderr, flush=True)
        res = run_bass_kernel_spmd(nc, in_maps, core_ids=core_ids)
        for c in range(len(core_ids)):
            carry[c].update(res.results[c])
        print(f"[run_phases] done {phs}", file=sys.stderr, flush=True)
    return carry


PLAN_UNFUSED = [
    (["A0", "A1a", "A1b"], set(), {"xn", "q0", "k0", "v0"}),
    (["B"], {"q0", "k0", "v0"}, {"o0"}),
    (["C1", "C2"], {"xn", "o0"}, {"x1"}),
    (["D"], {"x1"}, {"xn1", "q1", "k1", "v1", "mb1", "g1", "lg1"}),
    (["E"], {"k1", "v1", "mb1"}, {"nmem", "nnr"}),
    (["F"], {"q1", "nmem", "nnr"}, {"mh1", "dh1"}),
    (["G"], {"xn1", "lg1", "x1", "g1", "mh1", "dh1"}, {"out_fm"}),
]


PLAN_FUSED = [
    (["A0", "A1a", "A1b", "B", "C1", "C2", "D", "E", "F", "G"],
     set(), {"out_fm"}),
]


def kernel(**inputs):
    carry = run_phases(PLAN_FUSED, inputs)
    out = np.empty((B_ALL, S, D), dtype=np.float32)
    for c in range(NCORES):
        ofm = carry[c]["out_fm"]
        out[c * BPC:(c + 1) * BPC] = np.asarray(ofm).reshape(D, NT).T.reshape(BPC, S, D)
    return out


# version 10 (final)
# speedup vs baseline: 1.1218x; 1.1218x over previous
import contextlib
import numpy as np
import concourse.bass as bass
import concourse.mybir as mybir
from concourse.bass_utils import run_bass_kernel_spmd

F32 = mybir.dt.float32
BF16 = mybir.dt.bfloat16
AF = mybir.ActivationFunctionType
ALU = mybir.AluOpType
AX = mybir.AxisListType.X

NCORES = 8
B_ALL, S, D = 16, 2048, 2048
BPC = B_ALL // NCORES
NT = BPC * S
DC = D // 128
NH, HD = 16, 128
FFN = 8192
FC = FFN // 128
KV1, HD1 = 4, 128
FD1 = 768
FC1 = FD1 // 128
NE, NR = 8, 8
MOE_SCALE = 16.0 / 8.0
EPS1 = 1e-8
LN_EPS = 1e-5
NEG_BIG = -1.0e9

ST = 512
COLL_CORES = NCORES
NST = NT // ST
INV_SQRT_HD = float(1.0 / np.sqrt(128.0))



def _rope_tables():
    half = HD // 2
    inv_freq = 1.0 / (10000.0 ** (np.arange(half, dtype=np.float64) / half))
    t = np.arange(S, dtype=np.float64)
    fr = inv_freq[:, None] * t[None, :]
    return (np.cos(fr).astype(np.float32), np.sin(fr).astype(np.float32),
            (np.cos(fr) * INV_SQRT_HD).astype(np.float32),
            (np.sin(fr) * INV_SQRT_HD).astype(np.float32))


def _perm_evens_odds():
    p = np.concatenate([np.arange(0, HD, 2), np.arange(1, HD, 2)])
    return np.concatenate([h * HD + p for h in range(NH)])


def _causal_masks():
    m = np.zeros((4, 128, 512), dtype=np.float32)
    qi = np.arange(128)[:, None]
    kr = np.arange(512)[None, :]
    for s_ in range(4):
        m[s_] = np.where(kr <= s_ * 128 + qi, 0.0, NEG_BIG)
    return m


def prep_host(inputs):
    g = {}
    perm = _perm_evens_odds()
    col = lambda b: np.ascontiguousarray(np.asarray(b).reshape(-1, 128).T)
    g["wqT"] = np.ascontiguousarray(np.asarray(inputs["L0_attn_Wq"])[perm, :].T)
    g["bq"] = col(np.asarray(inputs["L0_attn_bq"])[perm])
    g["wkT"] = np.ascontiguousarray(np.asarray(inputs["L0_attn_Wk"])[perm, :].T)
    g["bk"] = col(np.asarray(inputs["L0_attn_bk"])[perm])
    g["wvT"] = np.ascontiguousarray(np.asarray(inputs["L0_attn_Wv"]).T)
    g["bv"] = np.asarray(inputs["L0_attn_bv"])
    g["woT"] = np.ascontiguousarray(np.asarray(inputs["L0_attn_Wo"]).T)
    g["bo"] = col(inputs["L0_attn_bo"])
    g["w0T"] = np.ascontiguousarray(np.asarray(inputs["L0_ffn_W0"]).T)
    g["b0"] = col(inputs["L0_ffn_bias0"])
    g["w1T"] = np.ascontiguousarray(np.asarray(inputs["L0_ffn_W1"]).T)
    g["b1"] = col(inputs["L0_ffn_bias1"])
    g["wq1T"] = np.ascontiguousarray(np.asarray(inputs["L1_attn_Wq"]).T)
    g["bq1"] = col(inputs["L1_attn_bq"])
    g["wk1T"] = np.ascontiguousarray(np.asarray(inputs["L1_attn_Wk"]).T)
    g["bk1"] = col(inputs["L1_attn_bk"])
    g["wv1T"] = np.ascontiguousarray(np.asarray(inputs["L1_attn_Wv"]).T)
    g["bv1"] = col(inputs["L1_attn_bv"])
    g["wgT"] = np.ascontiguousarray(np.asarray(inputs["L1_attn_Wg"]).T)
    g["bg"] = col(inputs["L1_attn_bg"])
    g["wmbT"] = np.ascontiguousarray(np.asarray(inputs["L1_attn_Wmb"]).T)
    g["bmb"] = col(inputs["L1_attn_bmb"])
    g["mem0"] = np.asarray(inputs["L1_attn_memory"])
    g["norm0T"] = np.ascontiguousarray(np.asarray(inputs["L1_attn_norm"]).T)
    g["wbT"] = np.ascontiguousarray(np.asarray(inputs["L1_ffn_W_base"]).T)
    g["bb"] = col(inputs["L1_ffn_b_base"])
    g["wrT"] = np.ascontiguousarray(np.asarray(inputs["L1_ffn_W_router"]).T)
    aw = np.asarray(inputs["L1_ffn_A_w"])
    g["waT"] = np.ascontiguousarray(aw.reshape(NE * NR, D).T)
    bw = np.asarray(inputs["L1_ffn_B_w"])
    g["wbcT"] = np.ascontiguousarray(bw.transpose(0, 2, 1).reshape(NE * NR, D))
    cos, sin, cq, sq = _rope_tables()
    g["c_k"] = cos; g["s_k"] = sin; g["c_q"] = cq; g["s_q"] = sq
    g["ident"] = np.eye(128, dtype=np.float32)
    g["cmask"] = _causal_masks()
    g["oh864"] = np.kron(np.eye(NE), np.ones((1, NR))).astype(np.float32)

    x = np.asarray(inputs["L0_attn_x"])
    per_core = []
    for c in range(NCORES):
        xc = x[c * BPC:(c + 1) * BPC]
        xfm = np.ascontiguousarray(xc.reshape(NT, D).T).reshape(DC, 128, NT)
        per_core.append({"x_fm": xfm})
    return g, per_core



SEM_NAMES = ("t", "v", "s", "g", "dS", "dG", "dC", "cc")
DMA_SEMS = {"dS": "sync", "dG": "gpsimd", "dC": "scalar"}
ENG_SEM = {"tensor": "t", "vector": "v", "scalar": "s", "gpsimd": "g"}


def fm3(dram_ap, t0, t1, c0=0, c1=None):
    if c1 is None:
        c1 = dram_ap.shape[0]
    return dram_ap[c0:c1, :, t0:t1].transpose([1, 0, 2])


def sb3(sbuf_ap, nch, width=None):
    return sbuf_ap.rearrange("p (c t) -> p c t", c=nch)


def merge_w(*wait_lists):
    d = {}
    for wl in wait_lists:
        for (s_, v_) in wl:
            cur = d.get(s_, 0)
            if cur is None:
                continue
            if v_ is None or v_ > cur:
                d[s_] = v_
    return list(d.items())


class Prog:

    def __init__(self):
        self.stream = {e: [] for e in ("sync", "tensor", "vector", "scalar", "gpsimd")}
        self.n = {k: 0 for k in SEM_NAMES}
        self.handles = {}
        self.init_waits = []
        self.dma_hist = {}

    def op(self, eng, fname, *args, waits=(), inc=None, incby=None, chain=False, **kw):
        waits = [(s_, (self.n[s_] if v_ is None else v_)) for (s_, v_) in waits]
        waits = [(s_, v_) for (s_, v_) in waits if v_ > 0]
        if chain:
            own = ENG_SEM[eng]
            waits = merge_w(waits, [(own, self.n[own])])
        if inc is not None:
            if incby is None:
                incby = 16 if inc in DMA_SEMS else 1
            self.n[inc] += incby
        waits = tuple(waits)

        def fn(e, args=args, kw=kw, waits=waits, inc=inc, incby=incby):
            for (s_, v_) in waits:
                e.wait_ge(self.handles[s_], v_)
            ins = getattr(e, fname)(*args, **kw)
            if inc is not None:
                ins.then_inc(self.handles[inc], incby)
        self.stream[eng].append(fn)
        return self.n[inc] if inc is not None else None

    def dma(self, eng, out, in_, waits=(), **kw):
        sem = {"sync": "dS", "gpsimd": "dG", "scalar": "dC"}[eng]
        return self.op(eng, "dma_start", waits=waits, inc=sem, out=out, in_=in_, **kw)

    def barrier(self):
        for eng in ("tensor", "vector", "scalar", "gpsimd"):
            self.op(eng, "nop", chain=True, inc=ENG_SEM[eng])
        vals = dict(self.n)
        wl = tuple((k, v) for k, v in vals.items() if v > 0)
        for eng in ("sync", "tensor", "vector", "scalar", "gpsimd"):
            def fn(e, wl=wl):
                for (s_, v_) in wl:
                    e.wait_ge(self.handles[s_], v_)
            self.stream[eng].append(fn)
        return vals

    def realize(self, nc, es):
        for k in SEM_NAMES:
            self.handles[k] = es.enter_context(nc.semaphore(k))
        blk = es.enter_context(nc.Block())
        iw = tuple(self.init_waits)
        for eng in ("sync", "tensor", "vector", "scalar", "gpsimd"):
            ops = self.stream[eng]
            if not ops:
                continue

            def run(e, ops=ops, eng=eng):
                if eng != "vector":
                    for (s_, v_) in iw:
                        e.wait_ge(self.handles[s_], v_)
                for f in ops:
                    f(e)
            getattr(blk, eng)(run)


SCRATCH = {
    "xn":  ([DC, 128, NT], BF16),
    "q0":  ([NH, 128, NT], BF16),
    "k0":  ([NH, 128, NT], BF16),
    "v0":  ([NT, D], BF16),
    "o0":  ([DC, 128, NT], BF16),
    "h0":  ([FC, 128, NT], BF16),
    "w1b": ([FC, 128, D], BF16),
    "x1":  ([DC, 128, NT], F32),
    "xn1": ([DC, 128, NT], BF16),
    "q1":  ([NH, 128, NT], BF16),
    "k1":  ([KV1, 128, NT], BF16),
    "v1":  ([KV1, 128, NT], BF16),
    "mb1": ([NH, 128, NT], BF16),
    "g1":  ([DC, 128, NT], BF16),
    "lg1": ([8, NT], F32),
    "mh1": ([NH, 128, NT], BF16),
    "dh1": ([NH, NT], F32),
    "drcp": ([NH, NT], F32),
    "nmem": ([128, NH * FC1 * 128], BF16),
    "nnr": ([128, NH * FC1], BF16),
    "ccmi": ([128, NH * FD1], F32),
    "ccmo": ([128, NH * FD1], F32),
    "ccni": ([NH * FD1], F32),
    "ccno": ([NH * FD1], F32),
}

GLOBALS_SHAPE = {
    "x_fm": ([DC, 128, NT], F32),
    "wqT": ([D, D], F32), "bq": ([128, NH], F32),
    "wkT": ([D, D], F32), "bk": ([128, NH], F32),
    "wvT": ([D, D], F32), "bv": ([D], F32),
    "woT": ([D, D], F32), "bo": ([128, DC], F32),
    "w0T": ([D, FFN], F32), "b0": ([128, FC], F32),
    "w1T": ([FFN, D], F32), "b1": ([128, DC], F32),
    "wq1T": ([D, D], F32), "bq1": ([128, NH], F32),
    "wk1T": ([D, KV1 * HD1], F32), "bk1": ([128, KV1], F32),
    "wv1T": ([D, KV1 * HD1], F32), "bv1": ([128, KV1], F32),
    "wgT": ([D, D], F32), "bg": ([128, DC], F32),
    "wmbT": ([D, D], F32), "bmb": ([128, NH], F32),
    "mem0": ([16, FD1, HD1], F32), "norm0T": ([FD1, 16], F32),
    "wbT": ([D, D], F32), "bb": ([128, DC], F32),
    "wrT": ([D, NE], F32),
    "waT": ([D, NE * NR], F32),
    "wbcT": ([NE * NR, D], F32),
    "c_k": ([64, S], F32), "s_k": ([64, S], F32),
    "c_q": ([64, S], F32), "s_q": ([64, S], F32),
    "ident": ([128, 128], F32),
    "cmask": ([4, 128, 512], F32),
    "oh864": ([NE, NE * NR], F32),
}

PHASE_USES = {
    "A0": ["x_fm"],
    "A1a": ["wqT", "bq", "wkT", "bk", "c_q", "s_q", "c_k", "s_k"],
    "A1b": ["wvT", "bv"],
    "B": ["woT", "bo", "ident", "cmask"],
    "C1": ["w0T", "b0", "w1T"],
    "C2": ["b1", "x_fm"],
    "D": ["wq1T", "bq1", "wk1T", "bk1", "wv1T", "bv1", "wgT", "bg", "wmbT", "bmb",
          "wrT"],
    "E": ["mem0", "norm0T", "ident"],
    "F": ["mem0", "norm0T", "ident"],
    "G": ["wbT", "bb", "waT", "wbcT", "oh864", "ident"],
}


class Ctx:
    def __init__(self, nc, es, dram):
        self.nc = nc
        self.es = es
        self.dram = dram
        self.p = Prog()
        self.ps_S = es.enter_context(nc.psum_tensor("ps_S", [128, 2048], F32))
        self.ps_A = es.enter_context(nc.psum_tensor("ps_A", [128, 512], F32))
        self.ps_B = es.enter_context(nc.psum_tensor("ps_B", [128, 512], F32))
        self.ps_C = es.enter_context(nc.psum_tensor("ps_C", [128, 512], F32))
        self.ps_BF = es.enter_context(nc.psum_tensor("ps_BF", [128, 1024], BF16))
        self.ones_bf = es.enter_context(nc.sbuf_tensor("ones_bf", [128, 1], BF16))
        self.ones_f = es.enter_context(nc.sbuf_tensor("ones_f", [1, 128], F32))
        p = self.p
        p.op("vector", "memset", self.ones_bf[:, :], 1.0, inc="v")
        p.op("vector", "memset", self.ones_f[:, :], 1.0, inc="v")
        p.init_waits = [("v", p.n["v"])]



def emit_A0(cx, src_name="x_fm", dst_name="xn"):
    p, nc = cx.p, cx.nc
    x_fm, xn = cx.dram[src_name], cx.dram[dst_name]
    pfx = "a0" + dst_name
    with contextlib.ExitStack() as es:
        sb = lambda n, sh, dt: es.enter_context(nc.sbuf_tensor(pfx + n[2:], sh, dt))
        xf = [sb(f"a0_xf{i}", [128, DC * ST], F32) for i in range(2)]
        xc = [sb(f"a0_xc{i}", [128, DC * ST], BF16) for i in range(2)]
        xq = [sb(f"a0_xq{i}", [128, DC * ST], BF16) for i in range(2)]
        xo = [sb(f"a0_xo{i}", [128, DC * ST], BF16) for i in range(2)]
        rowsA = [sb(f"a0_rows{i}", [1, 8 * ST], F32) for i in range(2)]
        b1s2 = [sb(f"a0_b1s{i}", [128, ST], F32) for i in range(2)]
        b2s2 = [sb(f"a0_b2s{i}", [128, ST], F32) for i in range(2)]

        war = {k: {} for k in ("xf", "xc", "xq", "xo")}
        prev = {}
        for i in range(NST):
            par = i % 2
            t0 = i * ST
            rows = rowsA[par]
            b1s, b2s = b1s2[par], b2s2[par]
            mean = rows[0:1, 0 * ST:1 * ST]
            e2p = rows[0:1, 1 * ST:2 * ST]
            m2 = rows[0:1, 2 * ST:3 * ST]
            varp = rows[0:1, 3 * ST:4 * ST]
            sstd = rows[0:1, 4 * ST:5 * ST]
            rstd = rows[0:1, 5 * ST:6 * ST]
            nmr = rows[0:1, 6 * ST:7 * ST]
            ps_sum = cx.ps_A[0:1, 0:ST] if par == 0 else cx.ps_B[0:1, 0:ST]
            ps_sq = cx.ps_C[0:1, 0:ST] if par == 0 else cx.ps_S[0:1, 0:ST]
            ps_b1 = cx.ps_S[:, 512:1024] if par == 0 else cx.ps_S[:, 1536:2048]
            ps_b2 = cx.ps_S[:, 1024:1536] if par == 0 else cx.ps_A[:, 0:512]
            src = fm3(x_fm, t0, t0 + ST)
            ldf = p.dma("sync", sb3(xf[par][:, :], DC), src, waits=war["xf"].get(par, []))
            ldc = p.dma("gpsimd", sb3(xc[par][:, :], DC), src, waits=war["xc"].get(par, []))
            sq = p.op("scalar", "activation", xq[par][:, :], xc[par][:, :], AF.Square,
                      waits=merge_w([("dG", ldc)], war["xq"].get(par, [])),
                      chain=True, inc="s")
            for c in range(DC):
                p.op("tensor", "matmul", ps_sum, cx.ones_bf[:, 0:1],
                     xc[par][:, c * ST:(c + 1) * ST],
                     waits=(merge_w([("dG", ldc)], prev.get(("st", par), []))
                            if c == 0 else []),
                     start=(c == 0), stop=(c == DC - 1),
                     inc=("t" if c == DC - 1 else None))
            tsum = p.n["t"]
            for c in range(DC):
                p.op("tensor", "matmul", ps_sq, cx.ones_bf[:, 0:1],
                     xq[par][:, c * ST:(c + 1) * ST],
                     waits=([("s", sq)] if c == 0 else []),
                     start=(c == 0), stop=(c == DC - 1),
                     inc=("t" if c == DC - 1 else None))
            tsq = p.n["t"]
            war["xc"][par] = [("t", tsum), ("s", sq)]
            war["xq"][par] = [("t", tsq)]
            p.op("vector", "tensor_scalar_mul", mean, ps_sum, 1.0 / D,
                 waits=merge_w([("t", tsum)], prev.get(("rows", par), [])), inc="v")
            p.op("vector", "tensor_scalar", e2p, ps_sq, 1.0 / D, LN_EPS,
                 ALU.mult, ALU.add, waits=[("t", tsq)], inc="v")
            p.op("vector", "tensor_mul", m2, mean, mean, chain=True, inc="v")
            vv = p.op("vector", "tensor_sub", varp, e2p, m2, chain=True, inc="v")
            sv = p.op("scalar", "activation", sstd, varp, AF.Sqrt,
                      waits=[("v", vv)], chain=True, inc="s")
            rv = p.op("vector", "reciprocal", rstd, sstd, waits=[("s", sv)], inc="v")
            p.op("gpsimd", "tensor_mul", nmr, mean, rstd,
                 waits=[("v", rv), ("g", p.n["g"])], inc="g")
            gn = p.op("gpsimd", "tensor_scalar_mul", nmr, nmr, -1.0, chain=True, inc="g")
            prev[("rows", par)] = [("g", gn)]
            tb1 = p.op("tensor", "matmul", ps_b1, cx.ones_f[0:1, :], rstd,
                       waits=merge_w([("v", rv)], prev.get(("bc", par), [])),
                       start=True, stop=True, inc="t")
            tb2 = p.op("tensor", "matmul", ps_b2, cx.ones_f[0:1, :], nmr,
                       waits=[("g", gn)], start=True, stop=True, inc="t")
            e1 = p.op("vector", "tensor_copy", b1s[:, :], ps_b1,
                      waits=[("t", tb1)], inc="v")
            e2 = p.op("vector", "tensor_copy", b2s[:, :], ps_b2,
                      waits=[("t", tb2)], chain=True, inc="v")
            prev[("bc", par)] = [("v", e2)]
            prev[("st", par)] = [("v", e2)]
            for c in range(DC):
                eng = "vector" if c % 2 == 0 else "gpsimd"
                w0 = []
                if c < 2:
                    w0 = merge_w([("dS", ldf), ("v", e2)], war["xo"].get(par, []))
                p.op(eng, "tensor_mul", xo[par][:, c * ST:(c + 1) * ST],
                     xf[par][:, c * ST:(c + 1) * ST], b1s[:, :],
                     waits=w0, inc=ENG_SEM[eng])
                p.op(eng, "tensor_tensor", xo[par][:, c * ST:(c + 1) * ST],
                     xo[par][:, c * ST:(c + 1) * ST], b2s[:, :], ALU.add,
                     chain=True, inc=ENG_SEM[eng])
            fin = [("v", p.n["v"]), ("g", p.n["g"])]
            war["xf"][par] = fin
            dst = fm3(xn, t0, t0 + ST)
            stv = p.dma("sync", dst, sb3(xo[par][:, :], DC), waits=fin)
            war["xo"][par] = [("dS", None)]
        p.barrier()


def _emit_proj_featmajor(cx, name, wT_name, bias_name, out_name, nheads,
                         rope=None, epilogue="bias", xn_name="xn",
                         kchunks=DC, mstart=0, scale=None, per_m_hook=None):
    p, nc = cx.p, cx.nc
    wT, out_d = cx.dram[wT_name], cx.dram[out_name]
    with contextlib.ExitStack() as es:
        sb = lambda n, sh, dt: es.enter_context(nc.sbuf_tensor(n, sh, dt))
        xr = sb(f"{name}_x", [128, kchunks * NT], BF16)
        wb = [sb(f"{name}_w{i}", [128, kchunks * 128], BF16) for i in range(2)]
        ev = [sb(f"{name}_e{i}", [128, NT], BF16) for i in range(2)]
        ro = [sb(f"{name}_r{i}", [128, NT], BF16) for i in range(2)] if rope else None
        bias = sb(f"{name}_b", [128, nheads], F32)
        ctv = stv_ = 0
        if rope:
            ct = sb(f"{name}_c", [128, S], BF16)
            st_ = sb(f"{name}_s", [128, S], BF16)
            csrc = cx.dram["c_q" if rope == "q" else "c_k"]
            ssrc = cx.dram["s_q" if rope == "q" else "s_k"]
            p.dma("gpsimd", ct[0:64, :], csrc[:, :])
            p.dma("gpsimd", ct[64:128, :], csrc[:, :])
            p.dma("gpsimd", st_[0:64, :], ssrc[:, :])
            stv_ = p.dma("gpsimd", st_[64:128, :], ssrc[:, :])
            sc = sb(f"{name}_sc", [128, ST], BF16)
        ldb = p.dma("sync", bias[:, :], cx.dram[bias_name][:, mstart:mstart + nheads])
        ldx = p.dma("sync", sb3(xr[:, :], kchunks), fm3(cx.dram[xn_name], 0, NT))
        ep_eng = "scalar" if epilogue in ("sigmoid", "gelu") else "vector"
        ep_sem = ENG_SEM[ep_eng]
        w_war = {}
        ps_war = {}
        ev_war = {}
        ro_war = {}
        prev_mm = 0
        for m in range(nheads):
            par = m % 2
            m0 = (mstart + m) * 128
            wsrc = wT[:, m0:m0 + 128].rearrange("(c p) m -> p c m", p=128)
            ldw = p.dma("gpsimd", sb3(wb[par][:, :], kchunks), wsrc,
                        waits=merge_w(w_war.get(par, []), [("t", prev_mm)]))
            if per_m_hook is not None:
                per_m_hook(m)
            prev_mm = p.n["t"] + 1
            for i in range(NST):
                t0 = i * ST
                pb = i % 2
                psl = cx.ps_S[:, pb * 512:pb * 512 + 512]
                for c in range(kchunks):
                    w0 = []
                    if c == 0:
                        w0 = merge_w([("dG", None), ("dS", ldx)], ps_war.get(pb, []))
                    p.op("tensor", "matmul", psl, wb[par][:, c * 128:(c + 1) * 128],
                         xr[:, c * NT + t0:c * NT + t0 + ST], waits=w0,
                         start=(c == 0), stop=(c == kchunks - 1),
                         inc=("t" if c == kchunks - 1 else None))
                tmm = p.n["t"]
                w_war[par] = [("t", tmm)]
                dst = ev[par][:, t0:t0 + ST]
                wts = merge_w([("t", tmm), ("dS", None)], ev_war.get(par, []))
                if epilogue == "sigmoid":
                    epv = p.op("scalar", "activation", dst, psl, AF.Sigmoid,
                               bias=bias[:, m:m + 1], waits=wts, inc="s")
                elif epilogue == "gelu":
                    epv = p.op("scalar", "activation", dst, psl, AF.Gelu,
                               bias=bias[:, m:m + 1], waits=wts, inc="s")
                else:
                    epv = p.op("vector", "tensor_scalar_add", dst, psl,
                               bias[:, m:m + 1], waits=wts, inc=ep_sem)
                ps_war[pb] = [(ep_sem, epv)]
                if rope:
                    s0 = t0 % S
                    top_i = ev[par][0:64, t0:t0 + ST]
                    bot_i = ev[par][64:128, t0:t0 + ST]
                    top_o = ro[par][0:64, t0:t0 + ST]
                    bot_o = ro[par][64:128, t0:t0 + ST]
                    w_first = merge_w([(ep_sem, epv), ("dG", None)],
                                      ro_war.get(par, []))
                    v1 = p.op("vector", "tensor_mul", sc[0:64, :], bot_i,
                              st_[64:128, s0:s0 + ST], waits=w_first, chain=True,
                              inc="v")
                    p.op("vector", "tensor_mul", top_o, top_i,
                         ct[0:64, s0:s0 + ST], inc="v")
                    rv = p.op("vector", "tensor_sub", top_o, top_o, sc[0:64, :],
                              chain=True, inc="v")
                    p.op("gpsimd", "tensor_mul", sc[64:128, :], top_i,
                         st_[0:64, s0:s0 + ST], waits=w_first, chain=True, inc="g")
                    p.op("gpsimd", "tensor_mul", bot_o, bot_i,
                         ct[64:128, s0:s0 + ST], inc="g")
                    rg = p.op("gpsimd", "tensor_add", bot_o, bot_o, sc[64:128, :],
                              chain=True, inc="g")
            if rope:
                rope_f = [("v", p.n["v"]), ("g", p.n["g"])]
                ev_war[par] = rope_f
                stw = p.dma("sync", out_d[mstart + m, :, :], ro[par][:, :],
                            waits=rope_f)
                ro_war[par] = [("dS", None)]
            else:
                stw = p.dma("sync", out_d[mstart + m, :, :], ev[par][:, :],
                            waits=[(ep_sem, p.n[ep_sem])])
                ev_war[par] = [("dS", None)]
        p.barrier()


def emit_A1a(cx):
    _emit_proj_featmajor(cx, "qp", "wqT", "bq", "q0", NH, rope="q")
    _emit_proj_featmajor(cx, "kp", "wkT", "bk", "k0", NH, rope="k")


def emit_A1b(cx):
    p, nc = cx.p, cx.nc
    with contextlib.ExitStack() as es:
        sb = lambda n, sh, dt: es.enter_context(nc.sbuf_tensor(n, sh, dt))
        wv = sb("vp_w", [128, DC * D], BF16)
        xs = [sb(f"vp_x{i}", [128, DC * 128], BF16) for i in range(2)]
        evb = [sb(f"vp_e{i}", [128, D], BF16) for i in range(2)]
        bias = sb("vp_b", [1, D], F32)
        bbc = sb("vp_bb", [128, D], F32)
        ldw = p.dma("gpsimd", sb3(wv[:, :], DC),
                    cx.dram["wvT"].rearrange("(c p) m -> p c m", p=128))
        ldb = p.dma("sync", bias[0:1, :], cx.dram["bv"].rearrange("(o d) -> o d", o=1))
        bcv = 0
        for nb in range(4):
            tv = p.op("tensor", "matmul", cx.ps_C[:, 0:512], cx.ones_f[0:1, :],
                      bias[0:1, nb * 512:(nb + 1) * 512],
                      waits=[("dS", ldb), ("v", bcv)], start=True, stop=True, inc="t")
            bcv = p.op("vector", "tensor_copy", bbc[:, nb * 512:(nb + 1) * 512],
                       cx.ps_C[:, 0:512], waits=[("t", tv)], inc="v")
        vd = cx.dram["v0"]
        xn_d = cx.dram["xn"]
        x_war = {0: [("v", bcv)], 1: [("v", bcv)]}
        prev_ev = 0
        for i in range(NT // 128):
            par = i % 2
            t0 = i * 128
            xsrc = fm3(xn_d, t0, t0 + 128)
            ldx = p.dma("sync", sb3(xs[par][:, :], DC), xsrc, waits=x_war.get(par, []))
            for c in range(DC):
                for nb in range(4):
                    w0 = []
                    if c == 0 and nb == 0:
                        w0 = [("dG", ldw), ("dS", ldx), ("v", prev_ev)]
                    p.op("tensor", "matmul", cx.ps_S[:, nb * 512:(nb + 1) * 512],
                         xs[par][:, c * 128:(c + 1) * 128],
                         wv[:, c * D + nb * 512:c * D + (nb + 1) * 512],
                         waits=w0, start=(c == 0), stop=(c == DC - 1),
                         inc=("t" if (c == DC - 1 and nb == 3) else None))
            tmm = p.n["t"]
            x_war[par] = [("t", tmm)]
            prev_ev = p.op("vector", "scalar_tensor_tensor", evb[par][:, :],
                           cx.ps_S[:, 0:2048], 1.0, bbc[:, :], ALU.mult, ALU.add,
                           waits=merge_w([("t", tmm), ("v", bcv)],
                                         x_war.get(("e", par), [])), inc="v")
            stw = p.dma("sync", vd[t0:t0 + 128, :], evb[par][:, :],
                        waits=[("v", prev_ev)])
            x_war[("e", par)] = [("dS", None)]
        p.barrier()




def emit_B(cx):
    p, nc = cx.p, cx.nc
    q0d, k0d, v0d, o0d = cx.dram["q0"], cx.dram["k0"], cx.dram["v0"], cx.dram["o0"]
    with contextlib.ExitStack() as es:
        sb = lambda n, sh, dt: es.enter_context(nc.sbuf_tensor(n, sh, dt))
        qh = [sb(f"b_q{i}", [128, S], BF16) for i in range(2)]
        kh = [sb(f"b_k{i}", [128, S], BF16) for i in range(2)]
        vh = [sb(f"b_v{i}", [128, 16 * 128], BF16) for i in range(2)]
        Pt = sb("b_P", [128, 4 * S], BF16)
        PTs = [sb(f"b_pt{i}", [128, 512], BF16) for i in range(2)]
        rowsum = sb("b_rsum", [128, 16], F32)
        rssum = sb("b_rss", [128, 4], F32)
        rsr = sb("b_rsr", [128, 4], F32)
        rsrow = sb("b_rsrow", [1, 512], BF16)
        bcs = sb("b_bcs", [128, 512], BF16)
        cm = sb("b_cm", [128, 4 * 512], F32)
        idn = sb("b_idn", [128, 128], BF16)
        idnf = sb("b_idnf", [128, 128], F32)
        A_seq = sb("b_A", [128, NH * S], BF16)
        wo_b = [sb(f"b_wo{i}", [128, DC * 128], BF16) for i in range(2)]
        o_ev = [sb(f"b_oe{i}", [128, S], BF16) for i in range(2)]
        bo_t = sb("b_bo", [128, DC], F32)
        ones_row = sb("b_ones", [1, 128], BF16)
        sq_t = sb("b_sq", [128, S], BF16)
        mxr = sb("b_mxr", [1, 16], F32)
        nBt = sb("b_nB", [128, 1], F32)
        nbb = sb("b_nbb", [1, 1], F32)

        p.op("vector", "memset", ones_row[:, :], 1.0, inc="v")
        p.dma("sync", sb3(cm[:, :], 4), cx.dram["cmask"].transpose([1, 0, 2]))
        p.dma("gpsimd", idn[:, :], cx.dram["ident"][:, :])
        p.dma("sync", idnf[:, :], cx.dram["ident"][:, :])
        p.dma("sync", bo_t[:, :], cx.dram["bo"][:, :])

        ld_war = {}
        w_war = {}
        oe_war = {}
        sexp_bank = {}
        pt_read_done = 0
        bf_evac = {}
        bc_evac = 0
        att_evac = 0
        aseq_war = 0
        pts_war = {}
        prev_unit_mm = 0
        prev_tnb = 0
        vss_war = {}
        for b in range(BPC):
            for h in range(NH):
                par = (b * NH + h) % 2
                t0b = b * S
                ldq = p.dma("sync", qh[par][:, :], q0d[h, :, t0b:t0b + S],
                            waits=merge_w(ld_war.get(par, []),
                                          [("t", prev_unit_mm)]))
                p.dma("sync", kh[par][:, :], k0d[h, :, t0b:t0b + S])
                ldv = p.dma("sync", sb3(vh[par][:, :], 16),
                            v0d[t0b:t0b + S, h * 128:(h + 1) * 128]
                            .rearrange("(kt p) d -> p kt d", p=128))
                prev_unit_mm = p.n["t"] + 1
                sqv = p.op("scalar", "activation", sq_t[:, :], qh[par][:, :],
                           AF.Square, waits=[("dS", None), ("s", p.n["s"]),
                                             ("t", prev_tnb)],
                           inc="s")
                vred = 0
                for j4 in range(4):
                    tcs = p.op("tensor", "matmul", cx.ps_C[0:1, 0:512],
                               cx.ones_bf[:, 0:1],
                               sq_t[:, j4 * 512:(j4 + 1) * 512],
                               waits=merge_w([("s", sqv), ("v", vred)],
                                             [("v", bc_evac)] if j4 == 0 else []),
                               start=True, stop=True, inc="t")
                    vred = p.op("vector", "reduce_max", mxr[0:1, j4:j4 + 1],
                                cx.ps_C[0:1, 0:512], axis=AX,
                                waits=[("t", tcs), ("v", p.n["v"])], inc="v")
                sqv2 = p.op("scalar", "activation", sq_t[:, :], kh[par][:, :],
                            AF.Square, waits=[("t", p.n["t"]), ("s", p.n["s"])],
                            chain=True, inc="s")
                for j4 in range(4):
                    tcs = p.op("tensor", "matmul", cx.ps_C[0:1, 0:512],
                               cx.ones_bf[:, 0:1],
                               sq_t[:, j4 * 512:(j4 + 1) * 512],
                               waits=[("s", sqv2), ("v", vred)],
                               start=True, stop=True, inc="t")
                    vred = p.op("vector", "reduce_max", mxr[0:1, 4 + j4:5 + j4],
                                cx.ps_C[0:1, 0:512], axis=AX,
                                waits=[("t", tcs)], chain=True, inc="v")
                vm1 = p.op("vector", "reduce_max", nbb[0:1, 0:1],
                           mxr[0:1, 0:4], axis=AX, chain=True, inc="v")
                vm2 = p.op("vector", "reduce_max", mxr[0:1, 8:9],
                           mxr[0:1, 4:8], axis=AX, chain=True, inc="v")
                vm3 = p.op("vector", "tensor_tensor", nbb[0:1, 0:1],
                           nbb[0:1, 0:1], mxr[0:1, 8:9], ALU.mult,
                           chain=True, inc="v")
                sm3 = p.op("scalar", "activation", nbb[0:1, 0:1], nbb[0:1, 0:1],
                           AF.Sqrt, waits=[("v", vm3)], chain=True, inc="s")
                vm4 = p.op("vector", "tensor_scalar_mul", nbb[0:1, 0:1],
                           nbb[0:1, 0:1], -1.0, waits=[("s", sm3)], inc="v")
                tnb = p.op("tensor", "matmul", cx.ps_C[:, 0:1],
                           cx.ones_f[0:1, :], nbb[0:1, 0:1],
                           waits=[("v", vm4)], start=True, stop=True, inc="t")
                vnb = p.op("vector", "tensor_copy", nBt[:, 0:1],
                           cx.ps_C[:, 0:1], waits=[("t", tnb)], inc="v")
                prev_tnb = tnb
                for qb in range(4):
                    width = (qb + 1) * 512
                    nkt = (qb + 1) * 4
                    sexp = {}
                    vrse = 0
                    for qs in range(4):
                        qoff = qb * 512 + qs * 128
                        for seg in range(qb + 1):
                            w0 = [("s", sexp_bank.get(seg, 0))]
                            if qs == 0 and seg == 0:
                                w0 = merge_w(w0, [("dS", ldv)])
                            p.op("tensor", "matmul",
                                 cx.ps_S[:, seg * 512:(seg + 1) * 512],
                                 qh[par][:, qoff:qoff + 128],
                                 kh[par][:, seg * 512:(seg + 1) * 512],
                                 waits=w0, start=True, stop=True, inc="t")
                            tsc = p.n["t"]
                            if seg == qb:
                                vmask = p.op("vector", "tensor_tensor",
                                             cx.ps_S[:, seg * 512:(seg + 1) * 512],
                                             cx.ps_S[:, seg * 512:(seg + 1) * 512],
                                             cm[:, qs * 512:(qs + 1) * 512],
                                             ALU.add, waits=[("t", tsc)], inc="v")
                                ew = [("v", vmask), ("v", vnb),
                                      ("v", vss_war.get(qs, 0)),
                                      ("t", pt_read_done)]
                            else:
                                ew = [("t", tsc), ("v", vnb),
                                      ("v", vss_war.get(qs, 0)),
                                      ("t", pt_read_done)]
                            sexp_bank[seg] = p.op(
                                "scalar", "activation",
                                Pt[:, qs * S + seg * 512:qs * S + (seg + 1) * 512],
                                cx.ps_S[:, seg * 512:(seg + 1) * 512], AF.Exp,
                                bias=nBt[:, 0:1],
                                accum_out=rowsum[:, qs * 4 + seg:qs * 4 + seg + 1],
                                waits=ew, chain=True, inc="s")
                        sexp[qs] = sexp_bank[qb]
                        vss = p.op("vector", "reduce_sum", rssum[:, qs:qs + 1],
                                   rowsum[:, qs * 4:qs * 4 + qb + 1], axis=AX,
                                   waits=[("s", sexp[qs]), ("t", None)],
                                   chain=True, inc="v")
                        vss_war[qs] = vss
                        vr = p.op("vector", "reciprocal", rsr[:, qs:qs + 1],
                                  rssum[:, qs:qs + 1], chain=True, inc="v")
                        tr = p.op("tensor", "transpose",
                                  cx.ps_C[0:1, qs * 128:(qs + 1) * 128],
                                  rsr[:, qs:qs + 1], idnf[:, :],
                                  waits=[("v", vr), ("v", bc_evac)], inc="t")
                        vrse = p.op("vector", "tensor_copy",
                                    rsrow[0:1, qs * 128:(qs + 1) * 128],
                                    cx.ps_C[0:1, qs * 128:(qs + 1) * 128],
                                    waits=[("t", tr)], inc="v")
                    tbc = p.op("tensor", "matmul", cx.ps_C[:, 0:512],
                               ones_row[0:1, :], rsrow[0:1, :],
                               waits=[("v", vrse)], start=True, stop=True, inc="t")
                    bc_evac = p.op("vector", "tensor_copy", bcs[:, :],
                                   cx.ps_C[:, 0:512], waits=[("t", tbc)],
                                   chain=True, inc="v")
                    tav = 0
                    for kt in range(nkt):
                        ptp = kt % 2
                        for qs in range(4):
                            w0 = []
                            if qs == 0:
                                w0 = [("s", sexp[3]), ("v", bf_evac.get(ptp, 0)),
                                      ("s", bf_evac.get(("s", ptp), 0)),
                                      ("dG", None)]
                            p.op("tensor", "transpose",
                                 cx.ps_BF[:, ptp * 512 + qs * 128:
                                          ptp * 512 + (qs + 1) * 128],
                                 Pt[:, qs * S + kt * 128:qs * S + (kt + 1) * 128],
                                 idn[:, :], waits=w0,
                                 inc=("t" if qs == 3 else None))
                        ttr = p.n["t"]
                        eng = "vector" if kt % 2 == 0 else "scalar"
                        ev = p.op(eng,
                                  "tensor_copy" if eng == "vector" else "copy",
                                  PTs[ptp][:, :],
                                  cx.ps_BF[:, ptp * 512:(ptp + 1) * 512],
                                  waits=[("t", ttr), ("t", pts_war.get(ptp, 0))],
                                  chain=True, inc=ENG_SEM[eng])
                        if eng == "vector":
                            bf_evac[ptp] = ev
                            evw = ("v", ev)
                        else:
                            bf_evac[("s", ptp)] = ev
                            evw = ("s", ev)
                        w0 = [evw]
                        if kt == 0:
                            w0.append(("v", att_evac))
                        tav = p.op("tensor", "matmul", cx.ps_A[:, 0:512],
                                   vh[par][:, kt * 128:(kt + 1) * 128],
                                   PTs[ptp][:, :], waits=w0,
                                   start=(kt == 0), stop=(kt == nkt - 1), inc="t")
                        pts_war[ptp] = tav
                    att_evac = p.op("vector", "scalar_tensor_tensor",
                                    A_seq[:, h * S + qb * 512:h * S + qb * 512 + 512],
                                    cx.ps_A[:, 0:512], 1.0, bcs[:, :],
                                    ALU.mult, ALU.mult,
                                    waits=[("t", tav), ("t", aseq_war)],
                                    chain=True, inc="v")
                    pt_read_done = p.n["t"]
                ld_war[par] = [("t", p.n["t"])]
            aseq_ready = p.n["v"]
            vev = 0
            prev_wo_mm = 0
            for m in range(DC):
                wpar = m % 2
                p.dma("gpsimd", sb3(wo_b[wpar][:, :], DC),
                      cx.dram["woT"][:, m * 128:(m + 1) * 128]
                      .rearrange("(c p) m -> p c m", p=128),
                      waits=merge_w(w_war.get(wpar, []), [("t", prev_wo_mm)],
                                    [("v", aseq_ready)] if m == 0 else []))
                prev_wo_mm = p.n["t"] + 1
                for st in range(4):
                    pb = st % 2
                    psl = cx.ps_S[:, pb * 512:pb * 512 + 512]
                    for c in range(DC):
                        w0 = []
                        if c == 0:
                            w0 = [("dG", None), ("v", aseq_ready),
                                  ("v", oe_war.get(pb, 0)),
                                  ("s", sexp_bank.get(pb, 0)),
                                  ("s", sexp_bank.get(pb + 2, 0))]
                        p.op("tensor", "matmul", psl,
                             wo_b[wpar][:, c * 128:(c + 1) * 128],
                             A_seq[:, c * S + st * 512:c * S + st * 512 + 512],
                             waits=w0, start=(c == 0), stop=(c == DC - 1),
                             inc=("t" if c == DC - 1 else None))
                    tmm = p.n["t"]
                    vev = p.op("vector", "tensor_scalar_add",
                               o_ev[wpar][:, st * 512:(st + 1) * 512], psl,
                               bo_t[:, m:m + 1],
                               waits=[("t", tmm), ("dS", None)], inc="v")
                    oe_war[pb] = vev
                w_war[wpar] = [("t", p.n["t"])]
                p.dma("sync", o0d[m, :, b * S:(b + 1) * S], o_ev[wpar][:, :],
                      waits=[("v", vev)])
            aseq_war = p.n["t"]
            woev = p.n["v"]
            for pk in (0, 1):
                ld_war[pk] = merge_w(ld_war.get(pk, []), [("v", woev)])
            for seg in range(4):
                sexp_bank[seg] = 0
        p.barrier()


def emit_C1(cx):
    p, nc = cx.p, cx.nc
    with contextlib.ExitStack() as es:
        wstage = [es.enter_context(nc.sbuf_tensor(f"c1_ws{i}", [128, D], BF16))
                  for i in range(2)]
        st_war = {}

        def precast(m):
            par = m % 2
            ld = p.dma("gpsimd", wstage[par][:, :],
                       cx.dram["w1T"][m * 128:(m + 1) * 128, :],
                       waits=st_war.get(par, []))
            stv = p.dma("gpsimd", cx.dram["w1b"][m, :, :], wstage[par][:, :],
                        waits=[("dG", ld)])
            st_war[par] = [("dG", stv)]
        _emit_proj_featmajor(cx, "ffa", "w0T", "b0", "h0", FC,
                             epilogue="gelu", per_m_hook=precast)


def emit_C2(cx):
    p, nc = cx.p, cx.nc
    SB = 512
    with contextlib.ExitStack() as es:
        sb = lambda n, sh, dt: es.enter_context(nc.sbuf_tensor(n, sh, dt))
        hres = [sb(f"c2_h{i}", [128, FC * SB], BF16) for i in range(2)]
        w1l = [sb(f"c2_w{i}", [128, FC * 128], BF16) for i in range(2)]
        xf = [sb(f"c2_x{i}", [128, 2 * SB], F32) for i in range(2)]
        of = [sb(f"c2_o{i}", [128, 2 * SB], BF16) for i in range(2)]
        xo = [sb(f"c2_xo{i}", [128, 2 * SB], F32) for i in range(2)]
        x1t = [sb(f"c2_x1{i}", [128, 2 * SB], F32) for i in range(2)]
        b1t = sb("c2_b1", [128, DC], F32)
        p.dma("sync", b1t[:, :], cx.dram["b1"][:, :])
        w_war = {}
        io_war = {}
        ps_war = {}
        x1_war = {}
        h_war = {}
        prev_mm = 0
        for pair in range(4):
            t0 = pair * 2 * SB
            for hf in range(2):
                h_war[hf] = h_war.get(hf, [])
                p.dma("sync", sb3(hres[hf][:, :], FC),
                      fm3(cx.dram["h0"], t0 + hf * SB, t0 + (hf + 1) * SB),
                      waits=merge_w(h_war[hf], [("t", prev_mm)]))
            for m in range(DC):
                par = m % 2
                ldw = p.dma("gpsimd", sb3(w1l[par][:, :], FC),
                            cx.dram["w1b"][:, :, m * 128:(m + 1) * 128]
                            .transpose([1, 0, 2]),
                            waits=merge_w(w_war.get(par, []), [("t", prev_mm)]))
                p.dma("sync", xf[par][:, :],
                      cx.dram["x_fm"][m, :, t0:t0 + 2 * SB],
                      waits=io_war.get(par, []))
                p.dma("sync", of[par][:, :],
                      cx.dram["o0"][m, :, t0:t0 + 2 * SB])
                prev_mm = p.n["t"] + 1
                gxo = p.op("gpsimd", "tensor_tensor", xo[par][:, :], xf[par][:, :],
                           of[par][:, :], ALU.add,
                           waits=merge_w([("dS", None)], x1_war.get(par, []),
                                         [("g", p.n["g"])]),
                           inc="g")
                for hf in range(2):
                    pb = hf
                    psl = cx.ps_S[:, pb * 512:pb * 512 + 512]
                    for c in range(FC):
                        w0 = []
                        if c == 0:
                            w0 = [("dG", None), ("dS", None),
                                  ("v", ps_war.get(pb, 0))]
                        p.op("tensor", "matmul", psl,
                             w1l[par][:, c * 128:(c + 1) * 128],
                             hres[hf][:, c * SB:(c + 1) * SB],
                             waits=w0, start=(c == 0), stop=(c == FC - 1),
                             inc=("t" if c == FC - 1 else None))
                    tmm = p.n["t"]
                    vev = p.op("vector", "scalar_tensor_tensor",
                               x1t[par][:, hf * SB:(hf + 1) * SB], psl,
                               b1t[:, m:m + 1], xo[par][:, hf * SB:(hf + 1) * SB],
                               ALU.add, ALU.add,
                               waits=[("t", tmm), ("g", gxo)], inc="v")
                    ps_war[pb] = vev
                p.dma("sync", cx.dram["x1"][m, :, t0:t0 + 2 * SB], x1t[par][:, :],
                      waits=[("v", vev)])
                x1_war[par] = [("v", vev)]
                io_war[par] = [("g", gxo), ("dS", None)]
                w_war[par] = [("t", p.n["t"])]
            h_war[0] = h_war[1] = [("t", p.n["t"])]
        p.barrier()


def emit_D(cx):
    emit_A0(cx, src_name="x1", dst_name="xn1")
    _emit_proj_featmajor(cx, "q1p", "wq1T", "bq1", "q1", NH, xn_name="xn1")
    _emit_proj_featmajor(cx, "k1p", "wk1T", "bk1", "k1", KV1, xn_name="xn1")
    _emit_proj_featmajor(cx, "v1p", "wv1T", "bv1", "v1", KV1, xn_name="xn1")
    _emit_proj_featmajor(cx, "mbp", "wmbT", "bmb", "mb1", NH, epilogue="sigmoid",
                         xn_name="xn1")
    _emit_proj_featmajor(cx, "gp", "wgT", "bg", "g1", DC, epilogue="sigmoid",
                         xn_name="xn1")
    emit_router(cx)


def emit_router(cx):
    p, nc = cx.p, cx.nc
    with contextlib.ExitStack() as es:
        sb = lambda n, sh, dt: es.enter_context(nc.sbuf_tensor(n, sh, dt))
        xr = sb("rt_x", [128, DC * NT], BF16)
        wr = sb("rt_w", [128, DC * NE], BF16)
        lg = [sb(f"rt_l{i}", [8, 512], F32) for i in range(2)]
        ldx = p.dma("sync", sb3(xr[:, :], DC), fm3(cx.dram["xn1"], 0, NT))
        ldw = p.dma("gpsimd", sb3(wr[:, :], DC),
                    cx.dram["wrT"].rearrange("(c p) m -> p c m", p=128))
        ev_war = {}
        for i in range(NST):
            par = i % 2
            t0 = i * ST
            for c in range(DC):
                w0 = []
                if c == 0:
                    w0 = merge_w([("dG", ldw), ("dS", ldx)],
                                 [("v", ev_war.get("ps", 0))])
                p.op("tensor", "matmul", cx.ps_A[0:8, 0:512],
                     wr[:, c * NE:(c + 1) * NE],
                     xr[:, c * NT + t0:c * NT + t0 + ST],
                     waits=w0, start=(c == 0), stop=(c == DC - 1),
                     inc=("t" if c == DC - 1 else None))
            tmm = p.n["t"]
            vev = p.op("vector", "tensor_copy", lg[par][:, :], cx.ps_A[0:8, 0:512],
                       waits=merge_w([("t", tmm)], ev_war.get(par, [])), inc="v")
            ev_war["ps"] = vev
            stv = p.dma("sync", cx.dram["lg1"][:, t0:t0 + ST], lg[par][:, :],
                        waits=[("v", vev)])
            ev_war[par] = [("dS", None)]
        p.barrier()




def _dpfp_front(cx, p, name, src_head_ap, mk_tm, xx, idn=None, waits0=(),
                mk_war=0):
    ktm = xx["ktm"]
    xxa = xx["xx"]
    ev_prev = 0
    for g8 in range(8):
        jj = g8 % 2
        for k4 in range(4):
            st32 = g8 * 4 + k4
            p.op("tensor", "transpose",
                 cx.ps_BF[:, jj * 512 + k4 * 128:jj * 512 + (k4 + 1) * 128],
                 src_head_ap[:, st32 * 128:(st32 + 1) * 128], idn,
                 waits=(merge_w(list(waits0), [("v", ev_prev)])
                        if k4 == 0 else []),
                 inc=("t" if k4 == 3 else None))
        tv = p.n["t"]
        ev_prev = p.op("vector", "tensor_copy", ktm[:, g8 * 512:(g8 + 1) * 512],
                       cx.ps_BF[:, jj * 512:(jj + 1) * 512],
                       waits=[("t", tv)], inc="v")
    ktm3 = ktm[:, :].rearrange("p (s t) -> p s t", s=32)
    xx3 = xxa[:, :].rearrange("p (s t) -> p s t", s=32)
    mk3 = mk_tm[:, :].rearrange("p (s t) -> p s t", s=32)
    p.op("vector", "tensor_scalar_max", xx3[:, :, 0:128], ktm3, 0.0,
         chain=True, inc="v")
    vx = p.op("vector", "tensor_scalar", xx3[:, :, 128:256], ktm3, -1.0, 0.0,
              ALU.mult, ALU.max, chain=True, inc="v")
    j = 1
    p.op("vector", "tensor_tensor", mk3[:, :, 0:1],
         xx3[:, :, 0:1], xx3[:, :, 255:256], ALU.mult,
         waits=[("t", mk_war)], chain=True, inc="v")
    p.op("vector", "tensor_tensor", mk3[:, :, 1:256],
         xx3[:, :, 1:256], xx3[:, :, 0:255], ALU.mult,
         chain=True, inc="v")
    for gi, j in enumerate((2, 3)):
        off = (j - 1) * 256
        p.op("gpsimd", "tensor_tensor", mk3[:, :, off:off + j],
             xx3[:, :, 0:j], xx3[:, :, 256 - j:256], ALU.mult,
             waits=([("v", vx), ("t", mk_war), ("g", p.n["g"])]
                    if gi == 0 else []),
             chain=(gi > 0), inc="g")
        p.op("gpsimd", "tensor_tensor", mk3[:, :, off + j:off + 256],
             xx3[:, :, j:256], xx3[:, :, 0:256 - j], ALU.mult,
             chain=True, inc="g")
    return p.n["v"], p.n["g"]


def _dpfp_normalize(cx, p, mk_tm, trash, rows, want_mksq=False):
    sumsq, rstd = rows["sumsq"], rows["rstd"]
    for st in range(32):
        p.op("scalar", "activation", trash[:, 0:768],
             mk_tm[:, st * 768:(st + 1) * 768], AF.Square,
             accum_out=sumsq[:, st:st + 1],
             waits=([("v", p.n["v"]), ("g", p.n["g"])] if st == 0 else []),
             chain=True, inc="s")
    sq_all = p.n["s"]
    rv = p.op("vector", "reciprocal", rstd[:, :], sumsq[:, :],
              waits=[("s", sq_all)], inc="v")
    rs = p.op("scalar", "activation", rstd[:, :], rstd[:, :], AF.Sqrt,
              waits=[("v", rv)], chain=True, inc="s")
    for st in range(32):
        mk_sl = mk_tm[:, st * 768:(st + 1) * 768]
        nv = p.op("vector", "tensor_scalar_mul", mk_sl, mk_sl,
                  rstd[:, st:st + 1],
                  waits=([("s", rs)] if st == 0 else []), chain=True, inc="v")
        if want_mksq:
            p.op("scalar", "activation", trash[:, 0:768], mk_sl, AF.Square,
                 accum_out=rows["mksq"][:, st:st + 1],
                 waits=[("v", nv)], chain=True, inc="s")


def _dpfp_ft(cx, p, mk_tm, mk_ft, idn=None, pre_waits=()):
    mk_ft3 = sb3(mk_ft[:, :], FC1)
    ev2 = 0
    dn_g = p.n["g"]
    dn_v = p.n["v"]
    for st in range(32):
        for c in range(FC1):
            w0 = []
            if c == 0:
                evw = [ev2] if isinstance(ev2, tuple) else []
                w0 = merge_w([("v", dn_v), ("g", dn_g)] if st == 0 else [],
                             evw, list(pre_waits) if st == 0 else [])
            p.op("tensor", "transpose",
                 cx.ps_BF[:, c * 128:(c + 1) * 128],
                 mk_tm[:, st * 768 + c * 128:st * 768 + (c + 1) * 128], idn,
                 waits=w0, inc=("t" if c == FC1 - 1 else None))
        tv = p.n["t"]
        eng = "vector" if st % 2 == 0 else "scalar"
        fname = "tensor_copy" if eng == "vector" else "copy"
        ev2 = p.op(eng, fname,
                   mk_ft3[:, :, st * 128:(st + 1) * 128],
                   cx.ps_BF[:, 0:768].rearrange("p (c t) -> p c t", c=FC1),
                   waits=[("t", tv)], chain=True, inc=ENG_SEM[eng])
        ev2 = (ENG_SEM[eng], ev2)
    return p.n["v"], p.n["t"]


def _emit_dpfp(cx, p, es, name, src_head_ap, mk_tm, mk_ft, xx, trash, rows,
               want_mksq=False, idn=None, waits0=()):
    _dpfp_front(cx, p, name, src_head_ap, mk_tm, xx, idn=idn, waits0=waits0)
    _dpfp_normalize(cx, p, mk_tm, trash, rows, want_mksq=want_mksq)
    _dpfp_ft(cx, p, mk_tm, mk_ft, idn=idn)
    return p.n["v"], p.n["t"]


def emit_E(cx):
    p, nc = cx.p, cx.nc
    with contextlib.ExitStack() as es:
        sb = lambda n, sh, dt: es.enter_context(nc.sbuf_tensor(n, sh, dt))
        ksb = sb("e_k", [128, NT], BF16)
        ktm = sb("e_ktm", [128, NT], BF16)
        x2 = sb("e_xx", [128, 32 * 256], BF16)
        mk_tm = sb("e_mktm", [128, 32 * FD1], BF16)
        mk_ft = sb("e_mkft", [128, FC1 * NT], BF16)
        trash = sb("e_trash", [128, FD1], BF16)
        sumsq = sb("e_ssq", [128, 32], F32)
        rstd = sb("e_rstd", [128, 32], F32)
        mksq = sb("e_msq", [128, 32], F32)
        rkq = sb("e_rkq", [128, 32], F32)
        mem_sb = sb("e_mem", [128, FC1 * 128], BF16)
        normall = sb("e_norm", [128, FC1 * 16], BF16)
        mbs = sb("e_mb", [128, NT], BF16)
        vsb = sb("e_v", [128, NT], BF16)
        denr = [sb(f"e_denr{i}", [1, 512], F32) for i in range(2)]
        rrowf = [sb(f"e_rrowf{i}", [1, 512], F32) for i in range(2)]
        rrow = [sb(f"e_rrow{i}", [1, 512], BF16) for i in range(2)]
        bcr = [sb(f"e_bcr{i}", [128, 512], BF16) for i in range(2)]
        pvt = [sb(f"e_pv{i}", [128, 512], BF16) for i in range(2)]
        mvt = [sb(f"e_mv{i}", [128, 512], BF16) for i in range(2)]
        wmv = [sb(f"e_wmv{i}", [128, 512], BF16) for i in range(2)]
        wmv_tm = sb("e_wmvtm", [128, 512], BF16)
        dent = sb("e_dent", [128, 4], F32)
        ctmp = sb("e_ctmp", [128, 4], F32)
        coef = sb("e_coef", [128, 4], BF16)
        dev_t = sb("e_dev", [128, FD1], F32)
        dnr_acc = sb("e_dnr", [1, FD1], F32)
        idn = sb("e_idn", [128, 128], BF16)
        ones_row = sb("e_ones", [1, 128], BF16)
        one1f = sb("e_one1", [1, 1], F32)

        p.op("vector", "memset", ones_row[:, :], 1.0, inc="v")
        p.op("vector", "memset", one1f[:, :], 1.0, inc="v")
        p.dma("gpsimd", idn[:, :], cx.dram["ident"][:, :])
        p.dma("gpsimd", sb3(normall[:, :], FC1),
              cx.dram["norm0T"].rearrange("(c p) h -> p c h", p=128))

        prev_delta_ev = 0
        slot_war = {}
        bc_ps_e = [cx.ps_C[:, 0:512], cx.ps_S[:, 1024:1536]]
        import os as _os
        _EPART = _os.environ.get("E_PART", "all")
        for kh in range(KV1):
            ldk = p.dma("sync", ksb[:, :], cx.dram["k1"][kh, :, :],
                        waits=[("t", p.n["t"])])
            ldv = p.dma("sync", vsb[:, :], cx.dram["v1"][kh, :, :],
                        waits=[("g", p.n["g"])])
            vd, td = _emit_dpfp(cx, p, es, f"e{kh}", ksb, mk_tm, mk_ft,
                                {"ktm": ktm, "xx": x2}, trash,
                                {"sumsq": sumsq, "rstd": rstd, "mksq": mksq},
                                want_mksq=True, idn=idn[:, :],
                                waits0=[("dS", None), ("dG", None)])
            rkv = p.op("vector", "reciprocal", rkq[:, :], mksq[:, :],
                       waits=[("s", p.n["s"])], inc="v")
            if _EPART == "dpfp":
                continue
            for qhi in range(4):
                h = kh * 4 + qhi
                ldm = p.dma("gpsimd", sb3(mem_sb[:, :], FC1),
                            cx.dram["mem0"][h].rearrange("(c p) d -> p c d", p=128),
                            waits=[("t", p.n["t"])])
                ldmb = p.dma("sync", mbs[:, :], cx.dram["mb1"][h, :, :],
                             waits=[("g", p.n["g"]), ("t", p.n["t"])])
                dnacc_v = p.op("vector", "memset", dnr_acc[0:1, :], 0.0,
                               waits=[("dS", None)], chain=True, inc="v")
                first_delta = True
                if _EPART == "heads":
                    continue
                for st8 in range(NST):
                    t0 = st8 * ST
                    sl = st8 % 2
                    num_ps = cx.ps_A[:, 0:512] if sl == 0 else cx.ps_S[:, 1536:2048]
                    for c in range(FC1):
                        w0 = []
                        if c == 0:
                            w0 = [("dG", None), ("dS", None),
                                  ("v", slot_war.get(("n", sl), 0))]
                        p.op("tensor", "matmul", num_ps,
                             mem_sb[:, c * 128:(c + 1) * 128],
                             mk_ft[:, c * NT + t0:c * NT + t0 + ST],
                             waits=w0, start=(c == 0), stop=(c == FC1 - 1),
                             inc=("t" if c == FC1 - 1 else None))
                    tnum = p.n["t"]
                    for c in range(FC1):
                        p.op("tensor", "matmul", cx.ps_B[0:1, 0:512],
                             normall[:, c * 16 + h:c * 16 + h + 1],
                             mk_ft[:, c * NT + t0:c * NT + t0 + ST],
                             waits=([("v", slot_war.get("d", 0))] if c == 0 else []),
                             start=(c == 0), stop=(c == FC1 - 1),
                             inc=("t" if c == FC1 - 1 else None))
                    tden = p.n["t"]
                    vde = p.op("vector", "tensor_scalar_add", denr[sl][0:1, :],
                               cx.ps_B[0:1, 0:512], EPS1,
                               waits=[("t", tden)], inc="v")
                    p.op("vector", "reciprocal", rrowf[sl][0:1, :],
                         denr[sl][0:1, :], chain=True, inc="v")
                    vrr = p.op("vector", "tensor_copy", rrow[sl][0:1, :],
                               rrowf[sl][0:1, :], chain=True, inc="v")
                    tbc = p.op("tensor", "matmul", bc_ps_e[sl], ones_row[0:1, :],
                               rrow[sl][0:1, :],
                               waits=merge_w([("v", vrr)],
                                             [("v", slot_war.get(("b", sl), 0))]),
                               start=True, stop=True, inc="t")
                    vbc = p.op("vector", "tensor_copy", bcr[sl][:, :], bc_ps_e[sl],
                               waits=[("t", tbc)], chain=True, inc="v")
                    slot_war[("b", sl)] = vbc
                    vpv = p.op("vector", "scalar_tensor_tensor", pvt[sl][:, :],
                               num_ps, 1.0, bcr[sl][:, :],
                               ALU.mult, ALU.mult, chain=True, inc="v")
                    slot_war[("n", sl)] = vpv
                    gmv = p.op("gpsimd", "tensor_tensor", mvt[sl][:, :],
                               vsb[:, t0:t0 + ST], pvt[sl][:, :], ALU.subtract,
                               waits=[("v", vpv), ("dS", None), ("g", p.n["g"])],
                               inc="g")
                    gwm = p.op("gpsimd", "tensor_tensor", wmv[sl][:, :],
                               mvt[sl][:, :], mbs[:, t0:t0 + ST], ALU.mult,
                               chain=True, inc="g")
                    for j4 in range(4):
                        p.op("tensor", "matmul", cx.ps_B[:, 4 + j4:5 + j4],
                             denr[sl][0:1, j4 * 128:(j4 + 1) * 128],
                             one1f[0:1, 0:1],
                             waits=([("v", vrr)] if j4 == 0 else []),
                             start=True, stop=True,
                             inc=("t" if j4 == 3 else None))
                    tdtm = p.n["t"]
                    vdt = p.op("vector", "tensor_copy", dent[:, :],
                               cx.ps_B[:, 4:8], waits=[("t", tdtm)], inc="v")
                    slot_war["d"] = vdt
                    vc1 = p.op("vector", "tensor_tensor", ctmp[:, :], dent[:, :],
                               rkq[:, st8 * 4:st8 * 4 + 4], ALU.mult,
                               chain=True, inc="v")
                    vc2 = p.op("vector", "tensor_scalar", coef[:, :], ctmp[:, :],
                               -1.0, 1.0, ALU.mult, ALU.add, chain=True, inc="v")
                    vc3 = p.op("vector", "tensor_scalar", coef[:, :], coef[:, :],
                               0.0, 1.0, ALU.max, ALU.min, chain=True, inc="v")
                    if _EPART == "numden":
                        continue
                    for k32 in range(4):
                        st32 = st8 * 4 + k32
                        last_d = (st8 == NST - 1 and k32 == 3)
                        ttw = p.op("tensor", "transpose", cx.ps_BF[:, 0:128],
                                   wmv[sl][:, k32 * 128:(k32 + 1) * 128], idn[:, :],
                                   waits=[("g", gwm), ("v", p.n["v"])], inc="t")
                        vtw = p.op("vector", "tensor_copy", wmv_tm[:, 0:128],
                                   cx.ps_BF[:, 0:128], waits=[("t", ttw)], inc="v")
                        p.op("tensor", "matmul", cx.ps_S[:, 0:512],
                             wmv_tm[:, 0:128],
                             mk_tm[:, st32 * 768:st32 * 768 + 512],
                             waits=merge_w([("v", vtw)],
                                           [("v", prev_delta_ev)]
                                           if first_delta else []),
                             start=first_delta, stop=last_d)
                        p.op("tensor", "matmul", cx.ps_S[:, 512:768],
                             wmv_tm[:, 0:128],
                             mk_tm[:, st32 * 768 + 512:st32 * 768 + 768],
                             start=first_delta, stop=last_d,
                             inc=("t" if last_d else None))
                        dn_a = bc_ps_e[sl][0:1, 0:512]
                        dn_b = bc_ps_e[1 - sl][0:1, 0:256]
                        p.op("tensor", "matmul", dn_a,
                             coef[:, k32:k32 + 1],
                             mk_tm[:, st32 * 768:st32 * 768 + 512],
                             waits=(merge_w([("v", vc3)], [("v", dnacc_v)])
                                    if k32 == 0 else []),
                             start=(k32 == 0), stop=(k32 == 3))
                        p.op("tensor", "matmul", dn_b,
                             coef[:, k32:k32 + 1],
                             mk_tm[:, st32 * 768 + 512:st32 * 768 + 768],
                             start=(k32 == 0), stop=(k32 == 3),
                             inc=("t" if k32 == 3 else None))
                        first_delta = False
                    tdn = p.n["t"]
                    p.op("vector", "tensor_tensor", dnr_acc[0:1, 0:512],
                         dnr_acc[0:1, 0:512], dn_a, ALU.add,
                         waits=[("t", tdn)], chain=True, inc="v")
                    dnacc_v = p.op("vector", "tensor_tensor", dnr_acc[0:1, 512:768],
                                   dnr_acc[0:1, 512:768], dn_b,
                                   ALU.add, chain=True, inc="v")
                    slot_war[("b", sl)] = dnacc_v
                    slot_war[("b", 1 - sl)] = dnacc_v
                if _EPART in ("heads", "numden"):
                    continue
                tdel = p.n["t"]
                ve1 = p.op("vector", "tensor_copy", dev_t[:, :],
                           cx.ps_S[:, 0:768], waits=[("t", tdel)], inc="v")
                p.dma("sync", cx.dram["ccmi"][:, h * FD1:(h + 1) * FD1],
                      dev_t[:, :], waits=[("v", ve1)])
                p.dma("sync",
                      cx.dram["ccni"].rearrange("(o f) -> o f", o=1)[0:1, h * FD1:(h + 1) * FD1],
                      dnr_acc[0:1, :], waits=[("v", dnacc_v)])
                prev_delta_ev = ve1
        p.barrier()
        rg = [list(range(COLL_CORES))]
        p.op("gpsimd", "collective_compute", "AllReduce", ALU.add, rg,
             [cx.dram["ccmi"][:, :]], [cx.dram["ccmo"][:, :]],
             waits=[("dS", None)], inc="cc", incby=1)
        p.op("gpsimd", "collective_compute", "AllReduce", ALU.add, rg,
             [cx.dram["ccni"].rearrange("(o f) -> o f", o=1)],
             [cx.dram["ccno"].rearrange("(o f) -> o f", o=1)],
             inc="cc", incby=1)
        p.barrier()
    with contextlib.ExitStack() as es2:
        sb = lambda n, sh, dt: es2.enter_context(nc.sbuf_tensor(n, sh, dt))
        dmr = sb("e2_dmr", [128, NH * FD1], F32)
        memall = sb("e2_mem", [128, NH * FC1 * 128], BF16)
        nm_sb = sb("e2_nm", [128, NH * FC1 * 128], BF16)
        dnrow = sb("e2_dnr", [1, NH * FD1], F32)
        normf = sb("e2_nf", [128, NH * FC1], F32)
        nnr_sb = sb("e2_nnr", [128, NH * FC1], BF16)
        idnf2 = sb("e2_idnf", [128, 128], F32)
        one1f2 = sb("e2_one1", [1, 1], F32)
        p.op("vector", "memset", one1f2[:, :], 1.0, inc="v")
        ld1 = p.dma("sync", dmr[:, :], cx.dram["ccmo"][:, :])
        ld2 = p.dma("sync", dnrow[0:1, :],
                    cx.dram["ccno"].rearrange("(o f) -> o f", o=1))
        ld3 = p.dma("gpsimd", memall[:, :].rearrange("p (h c d) -> p h c d", h=NH, c=FC1),
                    cx.dram["mem0"].rearrange("h (c p) d -> p h c d", p=128))
        ld4 = p.dma("sync", idnf2[:, :], cx.dram["ident"][:, :])
        ld5 = p.dma("gpsimd", sb3(normf[:, :], FC1),
                    cx.dram["norm0T"].rearrange("(c p) h -> p c h", p=128))
        for c in range(FC1):
            for h in range(NH):
                j = c * 16 + h
                p.op("tensor", "matmul", cx.ps_B[:, j:j + 1],
                     dnrow[0:1, h * FD1 + c * 128:h * FD1 + (c + 1) * 128],
                     one1f2[0:1, 0:1],
                     waits=([("dS", None), ("v", p.n["v"])] if j == 0 else []),
                     start=True, stop=True,
                     inc=("t" if (c == FC1 - 1 and h == NH - 1) else None))
        tnr = p.n["t"]
        vnr = p.op("vector", "tensor_tensor", nnr_sb[:, :], cx.ps_B[:, 0:96],
                   normf[:, :], ALU.add, waits=[("t", tnr), ("dG", None)], inc="v")
        p.dma("sync", cx.dram["nnr"][:, :], nnr_sb[:, :], waits=[("v", vnr)])
        vprev = 0
        for h in range(NH):
            for c in range(FC1):
                jj = (h * FC1 + c) % 2
                tt = p.op("tensor", "transpose",
                          cx.ps_C[:, jj * 128:(jj + 1) * 128],
                          dmr[:, h * FD1 + c * 128:h * FD1 + (c + 1) * 128],
                          idnf2[:, :],
                          waits=merge_w([("dS", None)] if (h == 0 and c == 0) else [],
                                        [("v", vprev)]),
                          inc="t")
                off = (h * FC1 + c) * 128
                vprev = p.op("vector", "tensor_tensor", nm_sb[:, off:off + 128],
                             cx.ps_C[:, jj * 128:(jj + 1) * 128],
                             memall[:, off:off + 128], ALU.add,
                             waits=[("t", tt), ("dG", None)], inc="v")
        p.dma("sync", cx.dram["nmem"][:, :], nm_sb[:, :], waits=[("v", vprev)])
        p.barrier()


def emit_F(cx):
    p, nc = cx.p, cx.nc
    with contextlib.ExitStack() as es:
        sb = lambda n, sh, dt: es.enter_context(nc.sbuf_tensor(n, sh, dt))
        qsb0 = sb("f_q0", [128, NT], BF16)
        qsb = [qsb0, qsb0]
        qtm = sb("f_qtm", [128, NT], BF16)
        x2 = sb("f_xx", [128, 32 * 256], BF16)
        mq_tm = sb("f_mqtm", [128, 32 * FD1], BF16)
        mq_ft = [sb(f"f_mqft{i}", [128, FC1 * NT], BF16) for i in range(2)]
        nm_h = [sb(f"f_nm{i}", [128, FC1 * 128], BF16) for i in range(2)]
        nnr_sb = sb("f_nnr", [128, NH * FC1], BF16)
        denall = sb("f_denall", [1, NT], F32)
        mh_st = [sb(f"f_mh{i}", [128, 512], BF16) for i in range(2)]
        idn = sb("f_idn", [128, 128], BF16)
        p.dma("gpsimd", idn[:, :], cx.dram["ident"][:, :])
        p.dma("sync", nnr_sb[:, :], cx.dram["nnr"][:, :])
        slot_war = {}
        ft_war = {0: 0, 1: 0}
        mk_war = 0
        q_war = {0: [], 1: []}
        prev_first_mm = 0
        for qh in range(NH + 1):
            if qh < NH:
                par = qh % 2
                ldq = p.dma("sync", qsb[par][:, :], cx.dram["q1"][qh, :, :],
                            waits=merge_w(q_war[par], [("t", prev_first_mm)]))
                prev_first_mm = p.n["t"] + 1
                _dpfp_front(cx, p, f"f{qh}", qsb[par], mq_tm,
                            {"ktm": qtm, "xx": x2}, idn=idn[:, :],
                            waits0=[("dS", None), ("dG", None), ("v", None),
                                    ("s", None)],
                            mk_war=mk_war)
                q_war[0] = q_war[1] = [("t", p.n["t"])]
            if qh > 0:
                qp = qh - 1
                slp = qp % 2
                ft = mq_ft[slp]
                p.dma("gpsimd", nm_h[slp][:, :],
                      cx.dram["nmem"][:, qp * FD1:(qp + 1) * FD1],
                      waits=[("t", None)])
                denall_v = 0
                for st8 in range(NST):
                    t0 = st8 * ST
                    sl = st8 % 2
                    num_ps = cx.ps_A[:, 0:512] if sl == 0 else cx.ps_S[:, 0:512]
                    den_ps = (cx.ps_B[0:1, 0:512] if sl == 0
                              else cx.ps_S[0:1, 512:1024])
                    for c in range(FC1):
                        w0 = []
                        if c == 0:
                            w0 = [("dS", None), ("dG", None),
                                  ("s", slot_war.get(("n", sl), 0))]
                        p.op("tensor", "matmul", num_ps,
                             nm_h[slp][:, c * 128:(c + 1) * 128],
                             ft[:, c * NT + t0:c * NT + t0 + ST],
                             waits=w0, start=(c == 0), stop=(c == FC1 - 1),
                             inc=("t" if c == FC1 - 1 else None))
                    tnum = p.n["t"]
                    for c in range(FC1):
                        p.op("tensor", "matmul", den_ps,
                             nnr_sb[:, c * 16 + qp:c * 16 + qp + 1],
                             ft[:, c * NT + t0:c * NT + t0 + ST],
                             waits=([("v", slot_war.get(("d", sl), 0))]
                                    if c == 0 else []),
                             start=(c == 0), stop=(c == FC1 - 1),
                             inc=("t" if c == FC1 - 1 else None))
                    tden = p.n["t"]
                    smh = p.op("scalar", "copy", mh_st[sl][:, :], num_ps,
                               waits=merge_w([("t", tnum)],
                                             [("dS", None)]
                                             if slot_war.get(("m", sl), 0) else []),
                               chain=True, inc="s")
                    slot_war[("n", sl)] = smh
                    stv = p.dma("sync", cx.dram["mh1"][qp, :, t0:t0 + ST],
                                mh_st[sl][:, :], waits=[("s", smh)])
                    slot_war[("m", sl)] = stv
                    denall_v = p.op("vector", "tensor_scalar_add",
                                    denall[0:1, t0:t0 + ST], den_ps, EPS1,
                                    waits=[("t", tden)], chain=True, inc="v")
                    slot_war[("d", sl)] = denall_v
                p.dma("sync",
                      cx.dram["dh1"].rearrange("(o h) t -> o (h t)", o=1)[0:1, qp * NT:(qp + 1) * NT],
                      denall[0:1, :], waits=[("v", denall_v)])
                ft_war[slp] = p.n["t"]
                slot_war[("nm", slp)] = p.n["t"]
            if qh < NH:
                _, tft = _dpfp_ft(cx, p, mq_tm, mq_ft[qh % 2], idn=idn[:, :],
                                  pre_waits=[("t", ft_war[qh % 2])])
                mk_war = tft
        p.barrier()


def emit_G(cx):
    p, nc = cx.p, cx.nc
    BIG = 1.0e30
    with contextlib.ExitStack() as es:
        sb = lambda n, sh, dt: es.enter_context(nc.sbuf_tensor(n, sh, dt))
        xr = sb("g_x", [128, DC * NT], BF16)
        w_fm = sb("g_wfm", [8, NT], BF16)
        Hs_sb = sb("g_Hs", [64, NT], BF16)
        wbc = sb("g_wbc", [64, D], BF16)
        bbt = sb("g_bb", [128, DC], F32)
        p.dma("sync", sb3(xr[:, :], DC), fm3(cx.dram["xn1"], 0, NT))
        p.dma("sync", bbt[:, :], cx.dram["bb"][:, :])
        p.dma("gpsimd", wbc[:, :], cx.dram["wbcT"][:, :])

        with contextlib.ExitStack() as es1:
            sb1 = lambda n, sh, dt: es1.enter_context(nc.sbuf_tensor(n, sh, dt))
            lgs = sb1("g_lg", [8, NT], F32)
            lgtm = sb1("g_lgtm", [128, 8], F32)
            m1t = sb1("g_m1", [128, 1], F32)
            nm1t = sb1("g_nm1", [128, 1], F32)
            mask1 = sb1("g_msk1", [128, 8], F32)
            t1t = sb1("g_t1", [128, 8], F32)
            tmpt = sb1("g_tmp", [128, 8], F32)
            m2t = sb1("g_m2", [128, 1], F32)
            mask2 = sb1("g_msk2", [128, 8], F32)
            pexp = sb1("g_pexp", [128, 8], F32)
            e2t = sb1("g_e2", [128, 1], F32)
            d1t = sb1("g_d1", [128, 1], F32)
            rt = sb1("g_r", [128, 1], F32)
            wt1 = sb1("g_wt1", [128, 8], F32)
            w_tm = sb1("g_wtm", [128, 32 * 8], BF16)
            idn = sb1("g_idn", [128, 128], BF16)
            idnf = sb1("g_idnf", [128, 128], F32)
            p.dma("sync", lgs[:, :], cx.dram["lg1"][:, :])
            p.dma("sync", idnf[:, :], cx.dram["ident"][:, :])
            p.dma("gpsimd", idn[:, :], cx.dram["ident"][:, :])
            vpw = 0
            for st in range(32):
                tlg = p.op("tensor", "transpose", cx.ps_C[:, 0:8],
                           lgs[0:8, st * 128:(st + 1) * 128], idnf[0:8, 0:8],
                           waits=merge_w([("dS", None), ("dG", None)]
                                         if st == 0 else [], [("v", vpw)]),
                           inc="t")
                vl = p.op("vector", "tensor_copy", lgtm[:, :], cx.ps_C[:, 0:8],
                          waits=[("t", tlg)], inc="v")
                vm1 = p.op("vector", "reduce_max", m1t[:, 0:1], lgtm[:, :],
                           axis=AX, chain=True, inc="v")
                vnm1 = p.op("vector", "reduce_max", nm1t[:, 0:1], lgtm[:, :],
                            axis=AX, negate=True, chain=True, inc="v")
                vmask1 = p.op("vector", "tensor_single_scalar", mask1[:, :],
                              lgtm[:, :], m1t[:, 0:1], ALU.is_ge, chain=True,
                              inc="v")
                gt1 = p.op("gpsimd", "tensor_scalar_mul", t1t[:, :], mask1[:, :],
                           -BIG, waits=[("v", vmask1), ("g", p.n["g"])], inc="g")
                gtmp = p.op("gpsimd", "tensor_tensor", tmpt[:, :], lgtm[:, :],
                            t1t[:, :], ALU.add, chain=True, inc="g")
                vm2 = p.op("vector", "reduce_max", m2t[:, 0:1], tmpt[:, :],
                           axis=AX, waits=[("g", gtmp)], inc="v")
                vmask2 = p.op("vector", "tensor_single_scalar", mask2[:, :],
                              lgtm[:, :], m2t[:, 0:1], ALU.is_ge, chain=True,
                              inc="v")
                sp = p.op("scalar", "activation", pexp[:, :], lgtm[:, :], AF.Exp,
                          bias=nm1t[:, 0:1], waits=[("v", vmask2), ("s", p.n["s"])],
                          inc="s")
                se2 = p.op("scalar", "activation", e2t[:, 0:1], m2t[:, 0:1],
                           AF.Exp, bias=nm1t[:, 0:1], chain=True, inc="s")
                vd1 = p.op("vector", "tensor_scalar_add", d1t[:, 0:1],
                           e2t[:, 0:1], 1.0, waits=[("s", se2)], inc="v")
                vr = p.op("vector", "reciprocal", rt[:, 0:1], d1t[:, 0:1],
                          chain=True, inc="v")
                vw1 = p.op("vector", "tensor_tensor", wt1[:, :], pexp[:, :],
                           mask2[:, :], ALU.mult, waits=[("s", sp)], chain=True,
                           inc="v")
                vw2 = p.op("vector", "tensor_scalar_mul",
                           w_tm[:, st * 8:(st + 1) * 8], wt1[:, :], rt[:, 0:1],
                           chain=True, inc="v")
                tw = p.op("tensor", "transpose", cx.ps_BF[0:8, 0:128],
                          w_tm[:, st * 8:(st + 1) * 8], idn[:, :],
                          waits=[("v", vw2)], inc="t")
                vpw = p.op("vector", "tensor_copy",
                           w_fm[0:8, st * 128:(st + 1) * 128],
                           cx.ps_BF[0:8, 0:128], waits=[("t", tw)], inc="v")
            wdone = vpw

        with contextlib.ExitStack() as es2:
            sb2 = lambda n, sh, dt: es2.enter_context(nc.sbuf_tensor(n, sh, dt))
            H_sb = sb2("g_H", [64, NT], BF16)
            wa = sb2("g_wa", [128, DC * 64], BF16)
            oh = sb2("g_oh", [8, NE * NR], BF16)
            ldo = p.dma("gpsimd", oh[:, :], cx.dram["oh864"][:, :],
                        waits=[("t", p.n["t"])])
            ldwa = p.dma("gpsimd", sb3(wa[:, :], DC),
                         cx.dram["waT"].rearrange("(c p) m -> p c m", p=128))
            hprev = 0
            for i in range(NST):
                t0 = i * ST
                for c in range(DC):
                    w0 = []
                    if c == 0:
                        w0 = [("dG", None), ("dS", None), ("v", hprev),
                              ("v", wdone)]
                    p.op("tensor", "matmul", cx.ps_A[0:64, 0:512],
                         wa[:, c * 64:(c + 1) * 64],
                         xr[:, c * NT + t0:c * NT + t0 + ST],
                         waits=w0, start=(c == 0), stop=(c == DC - 1),
                         inc=("t" if c == DC - 1 else None))
                th = p.n["t"]
                hprev = p.op("vector", "tensor_copy", H_sb[0:64, t0:t0 + ST],
                             cx.ps_A[0:64, 0:512], waits=[("t", th)], inc="v")
            hsprev = 0
            for i in range(NST):
                t0 = i * ST
                tb = p.op("tensor", "matmul", cx.ps_C[0:64, 0:512], oh[0:8, :],
                          w_fm[0:8, t0:t0 + ST],
                          waits=merge_w([("v", wdone), ("dG", None)]
                                        if i == 0 else [], [("v", hsprev)]),
                          start=True, stop=True, inc="t")
                hsprev = p.op("vector", "scalar_tensor_tensor",
                              Hs_sb[0:64, t0:t0 + ST], cx.ps_C[0:64, 0:512],
                              MOE_SCALE, H_sb[0:64, t0:t0 + ST],
                              ALU.mult, ALU.mult,
                              waits=[("t", tb), ("v", hprev)], chain=True,
                              inc="v")
            hs_done = hsprev

        HB = NT // 2
        wb_ = [sb(f"g_wb{i}", [128, DC * 128], BF16) for i in range(2)]
        x1l = [sb(f"g_x1{i}", [128, HB], F32) for i in range(2)]
        gl0 = sb("g_g0", [128, HB], BF16)
        gl = [gl0, gl0]
        mhl0 = sb("g_mh0", [128, HB], BF16)
        mhl = [mhl0, mhl0]
        asm = [sb(f"g_asm{i}", [128, HB], F32) for i in range(2)]
        dpc0 = sb("g_dpc0", [16, 512], F32)
        dpc = [dpc0, dpc0]
        drb0 = sb("g_drb0", [1, HB], BF16)
        drb = [drb0, drb0]
        rbc0 = sb("g_rbc0", [128, HB], BF16)
        rbc = [rbc0, rbc0]
        onbf = sb("g_onbf", [1, 128], BF16)
        p.op("vector", "memset", onbf[:, :], 1.0, inc="v")
        dh1f = cx.dram["dh1"].rearrange("(o h) t -> o h t", o=1)
        drcf = cx.dram["drcp"].rearrange("(o h) t -> o h t", o=1)
        dprev = 0
        for i8 in range(NST):
            dp = i8 % 2
            ldd = p.dma("sync", dpc[dp][:, :],
                        dh1f[0, :, i8 * 512:(i8 + 1) * 512],
                        waits=merge_w([("v", dprev)],
                                      [("dS", None)] if i8 > 0 else [],
                                      [("v", wdone)] if i8 == 0 else []))
            dprev = p.op("vector", "reciprocal", dpc[dp][:, :], dpc[dp][:, :],
                         waits=[("dS", ldd)], chain=True, inc="v")
            p.dma("sync", drcf[0, :, i8 * 512:(i8 + 1) * 512], dpc[dp][:, :],
                  waits=[("v", dprev)])
        drcp_done = p.n["dS"]
        drw_war = 0
        gmh_war = 0
        w_war = {}
        io_war = {}
        ps_war = {}
        prev_mm = 0
        for m in range(DC):
            par = m % 2
            ldw = p.dma("gpsimd", sb3(wb_[par][:, :], DC),
                        cx.dram["wbT"][:, m * 128:(m + 1) * 128]
                        .rearrange("(c p) m -> p c m", p=128),
                        waits=merge_w(w_war.get(par, []), [("t", prev_mm)],
                                      [("v", hs_done)] if m < 2 else []))
            prev_mm0 = prev_mm
            for half in range(2):
                hp = half
                th0 = half * HB
                ldx1 = p.dma("sync", x1l[hp][:, :],
                             cx.dram["x1"][m, :, th0:th0 + HB],
                             waits=merge_w(io_war.get(hp, []),
                                           [("v", hs_done), ("t", prev_mm0),
                                            ("dS", None)]
                                           if m == 0 else []))
                p.dma("sync", gl[hp][:, :], cx.dram["g1"][m, :, th0:th0 + HB],
                      waits=[("g", gmh_war)])
                p.dma("sync", mhl[hp][:, :], cx.dram["mh1"][m, :, th0:th0 + HB],
                      waits=[("dS", None)])
                lddr = p.dma("gpsimd", drb[hp][0:1, :],
                             cx.dram["drcp"][m, th0:th0 + HB]
                             .rearrange("(o t) -> o t", o=1),
                             waits=merge_w([("dS", drcp_done)]
                                           if (m == 0 and half == 0) else [],
                                           [("t", drw_war)]))
                prev_mm = p.n["t"] + 1
                for ii in range(HB // ST):
                    bps = cx.ps_C[:, 0:512] if ii % 2 == 0 else cx.ps_A[:, 0:512]
                    tb2_ = p.op("tensor", "matmul", bps, onbf[0:1, :],
                                drb[hp][0:1, ii * ST:(ii + 1) * ST],
                                waits=merge_w([("dG", lddr)],
                                              [("v", ps_war.get(("r", ii % 2), 0))]),
                                start=True, stop=True, inc="t")
                    ps_war[("r", ii % 2)] = p.op(
                        "vector", "tensor_copy",
                        rbc[hp][:, ii * ST:(ii + 1) * ST], bps,
                        waits=[("t", tb2_)], chain=True, inc="v")
                rbc_done = p.n["v"]
                drw_war = p.n["t"]
                g1o = p.op("gpsimd", "tensor_tensor", asm[hp][:, :],
                           gl[hp][:, :], mhl[hp][:, :], ALU.mult,
                           waits=merge_w([("dS", None), ("g", p.n["g"])],
                                         io_war.get(("a", hp), [])),
                           inc="g")
                g1b = p.op("gpsimd", "tensor_tensor", asm[hp][:, :],
                           asm[hp][:, :], rbc[hp][:, :], ALU.mult,
                           waits=[("v", rbc_done)], chain=True, inc="g")
                gmh_war = g1o
                g2o = p.op("gpsimd", "tensor_tensor", asm[hp][:, :],
                           asm[hp][:, :], xr[:, m * NT + th0:m * NT + th0 + HB],
                           ALU.add, chain=True, inc="g")
                va = p.op("vector", "tensor_tensor", asm[hp][:, :],
                          asm[hp][:, :], x1l[hp][:, :], ALU.add,
                          waits=[("g", g2o), ("dS", None), ("v", p.n["v"])],
                          inc="v")
                vasm_last = 0
                for ii in range(HB // ST):
                    i = half * (HB // ST) + ii
                    t0 = i * ST
                    tl0 = ii * ST
                    pb = i % 2
                    psl = cx.ps_S[:, pb * 512:pb * 512 + 512]
                    for c in range(DC):
                        w0 = []
                        if c == 0:
                            w0 = [("dG", None), ("dS", None),
                                  ("v", ps_war.get(pb, 0))]
                        p.op("tensor", "matmul", psl,
                             wb_[par][:, c * 128:(c + 1) * 128],
                             xr[:, c * NT + t0:c * NT + t0 + ST],
                             waits=w0, start=(c == 0), stop=False)
                    p.op("tensor", "matmul", psl, wbc[:, m * 128:(m + 1) * 128],
                         Hs_sb[0:64, t0:t0 + ST],
                         waits=[("v", hs_done)], start=False, stop=True, inc="t")
                    tmm = p.n["t"]
                    v2 = p.op("vector", "scalar_tensor_tensor",
                              asm[hp][:, tl0:tl0 + ST], psl, bbt[:, m:m + 1],
                              asm[hp][:, tl0:tl0 + ST], ALU.add, ALU.add,
                              waits=merge_w([("t", tmm)], [("v", va)]),
                              chain=True, inc="v")
                    ps_war[pb] = v2
                    vasm_last = v2
                p.dma("sync", cx.dram["out_fm"][m, :, th0:th0 + HB],
                      asm[hp][:, :], waits=[("v", vasm_last)])
                io_war[hp] = [("v", vasm_last), ("dS", None)]
                io_war[("a", hp)] = [("dS", None)]
                io_war[("r", hp)] = [("g", g1b)]
            w_war[par] = [("t", p.n["t"])]
        p.barrier()



PHASES = {
    "A0": emit_A0,
    "A1a": emit_A1a,
    "A1b": emit_A1b,
    "B": emit_B,
    "C1": emit_C1,
    "C2": emit_C2,
    "D": emit_D,
    "E": emit_E,
    "F": emit_F,
    "G": emit_G,
}


def build_launch(phase_names, reads, writes):
    nc = bass.Bass("TRN2", target_bir_lowering=False)
    es = contextlib.ExitStack()
    dram = {}
    used_g = set()
    for ph in phase_names:
        used_g.update(PHASE_USES[ph])
    for name in sorted(used_g):
        shape, dt = GLOBALS_SHAPE[name]
        dram[name] = nc.dram_tensor(name, shape, dt, kind="ExternalInput").ap()
    for name, (shape, dt) in SCRATCH.items():
        if name in reads:
            kind = "ExternalInput"
        elif name in writes:
            kind = "ExternalOutput"
        else:
            kind = "Internal"
        dram[name] = nc.dram_tensor(name, shape, dt, kind=kind).ap()
    if "out_fm" in writes:
        dram["out_fm"] = nc.dram_tensor("out_fm", [DC, 128, NT], F32,
                                        kind="ExternalOutput").ap()
    cx = Ctx(nc, es, dram)
    for ph in phase_names:
        PHASES[ph](cx)
    cx.p.realize(nc, es)
    es.close()
    return nc, used_g


def run_phases(groups, inputs, core_ids=None):
    if core_ids is None:
        core_ids = list(range(NCORES))
    shared, per_core = prep_host(inputs)
    carry = [dict(pc) for pc in per_core]
    for (phs, reads, writes) in groups:
        nc, used_g = build_launch(phs, reads, writes)
        in_maps = []
        for c in range(len(core_ids)):
            m = {}
            for k in used_g:
                m[k] = carry[c][k] if k in carry[c] else shared[k]
            for k in reads:
                m[k] = carry[c][k]
            in_maps.append(m)
        import sys
        print(f"[run_phases] launching {phs}", file=sys.stderr, flush=True)
        res = run_bass_kernel_spmd(nc, in_maps, core_ids=core_ids)
        for c in range(len(core_ids)):
            carry[c].update(res.results[c])
        print(f"[run_phases] done {phs}", file=sys.stderr, flush=True)
    return carry


PLAN_UNFUSED = [
    (["A0", "A1a", "A1b"], set(), {"xn", "q0", "k0", "v0"}),
    (["B"], {"q0", "k0", "v0"}, {"o0"}),
    (["C1", "C2"], {"xn", "o0"}, {"x1"}),
    (["D"], {"x1"}, {"xn1", "q1", "k1", "v1", "mb1", "g1", "lg1"}),
    (["E"], {"k1", "v1", "mb1"}, {"nmem", "nnr"}),
    (["F"], {"q1", "nmem", "nnr"}, {"mh1", "dh1"}),
    (["G"], {"xn1", "lg1", "x1", "g1", "mh1", "dh1"}, {"out_fm"}),
]


PLAN_FUSED = [
    (["A0", "A1a", "A1b", "B", "C1", "C2", "D", "E", "F", "G"],
     set(), {"out_fm"}),
]


def kernel(**inputs):
    carry = run_phases(PLAN_FUSED, inputs)
    out = np.empty((B_ALL, S, D), dtype=np.float32)
    for c in range(NCORES):
        ofm = carry[c]["out_fm"]
        out[c * BPC:(c + 1) * BPC] = np.asarray(ofm).reshape(D, NT).T.reshape(BPC, S, D)
    return out
